# version 19
# baseline (speedup 1.0000x reference)
import contextlib
import numpy as np
import ml_dtypes
import concourse.bass as bass
import concourse.mybir as mybir
from concourse.bass_utils import run_bass_kernel_spmd

F32 = mybir.dt.float32
BF16 = mybir.dt.bfloat16
AF = mybir.ActivationFunctionType
ALU = mybir.AluOpType
AX = mybir.AxisListType
BF = ml_dtypes.bfloat16

D = 1024
T = 2048
NSLAB = 8
NTB = 16
NTC = 4
HEADS = 16
NE = 18
FI = 1408
FIC = 11
EPS_LN = 1e-5
EPS_RMS = 1.1920929e-07
NEG = -1e30
_UPTO = 4
_STAGE = 99
_DBG = False
_SIMPLE_ROPE = False
_NQB = 16
_ESEL = None


def sl(i, n):
    return slice(i * n, (i + 1) * n)


class Ctx:
    def __init__(self, nc, block, stack):
        self.nc = nc
        self.block = block
        self._waited = {}
        self.sem = {}
        for name in ("tS", "vS", "cS", "gS", "dL", "dW", "dC", "dO"):
            self.sem[name] = [stack.enter_context(nc.semaphore(name)), 0]

    def inc(self, name, instr, k=1):
        s = self.sem[name]
        instr.then_inc(s[0], k)
        s[1] += k
        return s[1]

    def n(self, name):
        return self.sem[name][1]

    def w(self, eng, engid, name, val=None):
        s = self.sem[name]
        if val is None:
            val = s[1]
        if val is None or val <= 0:
            return
        key = (engid, name)
        if self._waited.get(key, 0) >= val:
            return
        eng.wait_ge(s[0], val)
        self._waited[key] = val

    def dma_guard(self, eng, engid):
        for nm in ("tS", "vS", "cS", "gS"):
            self.w(eng, engid, nm)

    def barrier(self, engines=("t", "v", "c", "g", "s")):
        marks = [(nm, self.n(nm)) for nm in ("tS", "vS", "cS", "gS")]
        bl = self.block
        decs = {"t": (bl.tensor, "T"), "v": (bl.vector, "V"),
                "c": (bl.scalar, "C"), "g": (bl.gpsimd, "G"),
                "s": (bl.sync, "S")}
        for ename in engines:
            dec, eid = decs[ename]

            @dec
            def _(eng, eid=eid):
                for nm, val in marks:
                    self.w(eng, eid, nm, val)


def emit_norm(c, H, kind):
    bl = c.block
    X, XB, PS = H["X"], H["XB"], H["PS"]
    AB, BB = H["AB"], H.get("BB")
    sq, tmpA, tmpB = H["sq"], H.get("tmpA"), H.get("tmpB")
    srows = H["srows"]
    onesb, onesf = H["ones_col_bf"], H["ones_row_f32"]
    epsap = H["epsA"]

    copy_marks = [0] * NSLAB
    sq_marks = [0] * NSLAB

    if kind == "ln":
        @bl.vector
        def _(v):
            for s in range(NSLAB):
                copy_marks[s] = c.inc("vS",
                                      v.tensor_copy(XB[:, s, :], X[:, s, :]))

    @bl.scalar
    def _(sc):
        for s in range(NSLAB):
            sq_marks[s] = c.inc(
                "cS", sc.activation(sq[:, s, :], X[:, s, :], AF.Square))

    stats = []
    if kind == "ln":
        stats.append((XB, copy_marks, "vS", 0))
    stats.append((sq, sq_marks, "cS", 1))
    groups = [(sti, ch) for sti in range(len(stats)) for ch in range(NTC)]
    evac_marks = {}
    grp_marks = {}
    for gi, (sti, ch) in enumerate(groups):
        buf, marks, semn, row = stats[sti]
        bank = PS[gi % 2]

        @bl.tensor
        def _(t, ch=ch, gi=gi, buf=buf, marks=marks, semn=semn, bank=bank):
            if gi >= 2:
                c.w(t, "T", "vS", evac_marks[gi - 2])
            for s in range(NSLAB):
                c.w(t, "T", semn, marks[s])
                ii = t.matmul(bank[0:1, :], onesb[:], buf[:, s, sl(ch, 512)],
                              start=(s == 0), stop=(s == NSLAB - 1))
            grp_marks[gi] = c.inc("tS", ii)

        @bl.vector
        def _(v, gi=gi, row=stats[sti][3], ch=ch, bank=bank):
            c.w(v, "V", "tS", grp_marks[gi])
            evac_marks[gi] = c.inc(
                "vS", v.tensor_copy(srows[0:1, row, sl(ch, 512)],
                                    bank[0:1, :]))

    stats_done_t = c.n("tS")
    rows_done = c.n("vS")
    rm = {}

    if kind == "ln":
        bcl = [(0, tmpA), (1, tmpB)]
    else:
        bcl = [(1, AB)]
    bc_marks = {}
    bc_ev = {}
    for ri, (row, dst) in enumerate(bcl):
        for ch in range(NTC):
            gi = ri * NTC + ch
            bank = PS[2 + gi % 2]

            @bl.tensor
            def _(t, row=row, ch=ch, gi=gi, bank=bank):
                c.w(t, "T", "vS", rows_done)
                if gi >= 2:
                    c.w(t, "T", "vS", bc_ev[gi - 2])
                bc_marks[gi] = c.inc("tS", t.matmul(
                    bank[:], onesf[0:1, :], srows[0:1, row, sl(ch, 512)],
                    start=True, stop=True))

            @bl.vector
            def _(v, dst=dst, ch=ch, gi=gi, bank=bank):
                c.w(v, "V", "tS", bc_marks[gi])
                bc_ev[gi] = c.inc("vS", v.tensor_copy(dst[:, sl(ch, 512)],
                                                      bank[:]))
    bc_done = c.n("vS")

    if kind == "ln":
        @bl.vector
        def _(v):
            c.w(v, "V", "vS", bc_done)
            rm["m"] = c.inc("vS", v.tensor_scalar(
                tmpA[:], tmpA[:], 1.0 / D, None, ALU.mult))

        @bl.scalar
        def _(sc):
            c.w(sc, "C", "vS", rm["m"])
            rm["msq"] = c.inc("cS", sc.activation(AB[:], tmpA[:], AF.Square))

        @bl.vector
        def _(v):
            c.w(v, "V", "cS", rm["msq"])
            rm["var"] = c.inc("vS", v.scalar_tensor_tensor(
                tmpB[:], tmpB[:], 1.0 / D, AB[:], ALU.mult, ALU.subtract))

        @bl.scalar
        def _(sc):
            c.w(sc, "C", "vS", rm["var"])
            lnm = c.inc("cS", sc.activation(tmpB[:], tmpB[:], AF.Ln,
                                            bias=epsap))
            c.w(sc, "C", "cS", lnm)
            rm["a"] = c.inc("cS", sc.activation(
                AB[:], tmpB[:], AF.Exp, scale=-0.5))

        @bl.vector
        def _(v):
            c.w(v, "V", "cS", rm["a"])
            rm["b"] = c.inc("vS", v.scalar_tensor_tensor(
                BB[:], tmpA[:], -1.0, AB[:], ALU.mult, ALU.mult))
        ab_done = rm["b"]
    else:
        @bl.vector
        def _(v):
            c.w(v, "V", "vS", bc_done)
            rm["var"] = c.inc("vS", v.tensor_scalar(
                AB[:], AB[:], 1.0 / D, None, ALU.mult))

        @bl.scalar
        def _(sc):
            c.w(sc, "C", "vS", rm["var"])
            lnm = c.inc("cS", sc.activation(AB[:], AB[:], AF.Ln,
                                            bias=epsap))
            c.w(sc, "C", "cS", lnm)
            rm["a"] = c.inc("cS", sc.activation(
                AB[:], AB[:], AF.Exp, scale=-0.5))

        @bl.vector
        def _(v):
            c.w(v, "V", "cS", rm["a"])
            ab_done = None
        ab_done = c.n("cS")
    apply_marks = [0] * NSLAB

    @bl.vector
    def _(v):
        if kind == "ln":
            c.w(v, "V", "vS", ab_done)
        else:
            c.w(v, "V", "cS", ab_done)
        c.w(v, "V", "tS", stats_done_t)
        for s in range(5):
            if kind == "ln":
                v.tensor_tensor(tmpA[:], X[:, s, :], AB[:], ALU.mult)
                apply_marks[s] = c.inc("vS", v.tensor_tensor(
                    XB[:, s, :], tmpA[:], BB[:], ALU.add))
            else:
                apply_marks[s] = c.inc("vS", v.tensor_tensor(
                    XB[:, s, :], X[:, s, :], AB[:], ALU.mult))

    @bl.gpsimd
    def _(g):
        if kind == "ln":
            c.w(g, "G", "vS", ab_done)
        else:
            c.w(g, "G", "cS", ab_done)
        c.w(g, "G", "tS", stats_done_t)
        for s in range(5, NSLAB):
            if kind == "ln":
                gm1 = c.inc("gS", g.tensor_tensor(tmpB[:], X[:, s, :],
                                                  AB[:], ALU.mult))
                c.w(g, "G", "gS", gm1)
                apply_marks[s] = c.inc("gS", g.tensor_tensor(
                    XB[:, s, :], tmpB[:], BB[:], ALU.add))
            else:
                apply_marks[s] = c.inc("gS", g.tensor_tensor(
                    XB[:, s, :], X[:, s, :], AB[:], ALU.mult))

    return apply_marks


def _wait_xb(c, eng, engid, marks):
    c.w(eng, engid, "vS", max(marks[:5]))
    c.w(eng, engid, "gS", max(marks[5:]))


def build(upto=4, stage=None, dbg=None):
    if stage is None:
        stage = _STAGE
    if dbg is None:
        dbg = _DBG
    nc = bass.Bass("TRN2", target_bir_lowering=False)
    dr = {}

    def din(name, shape, dt=F32):
        dr[name] = nc.dram_tensor(name, shape, dt, kind="ExternalInput").ap()
        return dr[name]

    x_d = din("x", [D, T])
    wqkv_qk_d = din("wqkv_qk", [D, 2048])
    wqkv_v_d = din("wqkv_v", [D, D])
    qk_bias_d = din("qk_bias", [128, 16])
    v_bias_d = din("v_bias", [128, D])
    ropeC_d = din("ropeC", [128, T], BF16)
    ropeS_d = din("ropeS", [128, T], BF16)
    pe_d = din("pemats", [4, 128, 128], BF16)
    wo_d = din("wo", [D, D])
    wo_bias_d = din("wo_bias", [128, 8])
    mask_d = din("mask384", [128, 384])
    mask_bf_d = din("mask_bf", [128, 384], BF16)
    id_bf_d = din("id_bf", [128, 128], BF16)
    id_f32_d = din("id_f32", [128, 128])
    ones_col_bf_d = din("ones_col_bf", [128, 1], BF16)
    ones_row_bf_d = din("ones_row_bf", [128, 128], BF16)
    ones_row_f32_d = din("ones_row_f32", [128, 128])
    gwT_d = din("gwT", [D, 16])
    gbias_d = din("gbias", [128, 256])
    epsv_d = din("epsv", [128, 256])
    epsg_d = din("epsg", [128, 64])
    gpT_d = din("gpT", [NE, D, FI])
    upT_d = din("upT", [NE, D, FI])
    dnT_d = din("dnT", [NE, FI, D])
    wmT_d = din("wmT", [D, D])
    v0_d = din("v0t", [128, 2])
    v1_d = din("v1t", [128, 2])
    wpT_d = din("wpT", [512, D])
    os_d = din("os_t", [128, 8])
    epsc_d = din("epsc", [128, 2])

    out_d = nc.dram_tensor("out", [D, T], F32, kind="ExternalOutput").ap()
    if dbg:
        dbg_xb = nc.dram_tensor("dbg_xb", [128, NSLAB, T], BF16,
                                kind="ExternalOutput").ap()
        H_dbg = True
        dbg_qf = nc.dram_tensor("dbg_qf", [128, NSLAB, T], BF16,
                                kind="ExternalOutput").ap()
        dbg_kf = nc.dram_tensor("dbg_kf", [128, NSLAB, T], BF16,
                                kind="ExternalOutput").ap()
        dbg_ao = nc.dram_tensor("dbg_ao", [128, NSLAB, 128], BF16,
                                kind="ExternalOutput").ap()
        dbg_pt = nc.dram_tensor("dbg_pt", [128, 384], BF16,
                                kind="ExternalOutput").ap()
        dbg_pb = nc.dram_tensor("dbg_pb", [128, 384], BF16,
                                kind="ExternalOutput").ap()
        dbg_sc = nc.dram_tensor("dbg_sc", [128, 384], F32,
                                kind="ExternalOutput").ap()
        dbg_s4 = nc.dram_tensor("dbg_s4", [128, 4], F32,
                                kind="ExternalOutput").ap()
        dbg_cw = nc.dram_tensor("dbg_cw", [16, T], F32,
                                kind="ExternalOutput").ap()
        dbg_act = nc.dram_tensor("dbg_act", [128, FIC, 1024], BF16,
                                 kind="ExternalOutput").ap()
    cw_dram = nc.dram_tensor("cw_scratch", [16, T], F32, kind="Internal").ap()

    with contextlib.ExitStack() as stack:
        e = stack.enter_context
        X = e(nc.sbuf_tensor("X", [128, NSLAB, T], F32))
        XB = e(nc.sbuf_tensor("XB", [128, NSLAB, T], BF16))
        PS = [e(nc.psum_tensor(f"PS{i}", [128, 512], F32)) for i in range(8)]
        qk_bias = e(nc.sbuf_tensor("qk_bias_s", [128, 16], F32))
        wo_bias = e(nc.sbuf_tensor("wo_bias_s", [128, 8], F32))
        mask384 = e(nc.sbuf_tensor("mask384_s", [128, 384], F32))
        mask_bf = e(nc.sbuf_tensor("mask_bf_s", [128, 384], BF16))
        id_bf = e(nc.sbuf_tensor("id_bf_s", [128, 128], BF16))
        id_f32 = e(nc.sbuf_tensor("id_f32_s", [128, 128], F32))
        ones_col_bf = e(nc.sbuf_tensor("ones_col_bf_s", [128, 1], BF16))
        ones_row_bf = e(nc.sbuf_tensor("ones_row_bf_s", [128, 128], BF16))
        ones_row_f32 = e(nc.sbuf_tensor("ones_row_f32_s", [128, 128], F32))
        os_t = e(nc.sbuf_tensor("os_t_s", [128, 8], F32))
        epsc = e(nc.sbuf_tensor("epsc_s", [128, 2], F32))
        gw_sb = e(nc.sbuf_tensor("gw_sb", [128, 8, 16], BF16))
        gbias_sb = e(nc.sbuf_tensor("gbias_sb", [128, 16, 16], F32))
        epsv_sb = e(nc.sbuf_tensor("epsv_sb", [128, 16, 16], F32))
        epsg_sb = e(nc.sbuf_tensor("epsg_sb", [128, 16, 4], F32))

        block = e(nc.Block())
        c = Ctx(nc, block, stack)
        H = dict(X=X, XB=XB, PS=PS, ones_col_bf=ones_col_bf,
                 ones_row_f32=ones_row_f32, ones_row_bf=ones_row_bf,
                 epsc=epsc, dbg_cw=(dbg_cw if dbg else None),
                 dbg_act=(dbg_act if dbg else None))

        x_v = x_d.rearrange("(s p) t -> p s t", p=128)

        @block.scalar
        def _(sc):
            c.inc("dL", sc.dma_start(out=X[:, 3:6, :], in_=x_v[:, 3:6, :]),
                  16)

        @block.gpsimd
        def _(g):
            c.inc("dL", g.dma_start(out=X[:, 6:8, :], in_=x_v[:, 6:8, :]),
                  16)

        @block.sync
        def _(sy):
            c.inc("dL", sy.dma_start(out=X[:, 0:3, :], in_=x_v[:, 0:3, :]),
                  16)
            for dst, src in ((qk_bias, qk_bias_d), (wo_bias, wo_bias_d),
                             (mask384, mask_d), (id_bf, id_bf_d),
                             (mask_bf, mask_bf_d),
                             (id_f32, id_f32_d), (ones_col_bf, ones_col_bf_d),
                             (ones_row_f32, ones_row_f32_d), (os_t, os_d),
                             (ones_row_bf, ones_row_bf_d),
                             (epsc, epsc_d),
                             (gbias_sb, gbias_d.rearrange(
                                 "p (a b) -> p a b", b=16)),
                             (epsv_sb, epsv_d.rearrange(
                                 "p (a b) -> p a b", b=16)),
                             (epsg_sb, epsg_d.rearrange(
                                 "p (a b) -> p a b", b=4))):
                c.inc("dL", sy.dma_start(out=dst[:], in_=src), 16)

        gw_mark = [0]

        @block.gpsimd
        def _(g):
            gw_mark[0] = c.inc("dW", g.dma_start(
                out=gw_sb[:], in_=gwT_d.rearrange(
                    "(kc p) m -> p kc m", p=128)), 16)
        ld_consts = c.n("dL")

        for dec, eid in ((block.vector, "V"), (block.scalar, "C"),
                         (block.tensor, "T"), (block.gpsimd, "G")):
            @dec
            def _(eng, eid=eid):
                c.w(eng, eid, "dL", ld_consts)

        with contextlib.ExitStack() as stA:
            eA = stA.enter_context
            AB1 = eA(nc.sbuf_tensor("AB1", [128, T], F32))
            BB1 = eA(nc.sbuf_tensor("BB1", [128, T], F32))
            sq1 = eA(nc.sbuf_tensor("sq1", [128, NSLAB, T], BF16))
            tmpA1 = eA(nc.sbuf_tensor("tmpA1", [128, T], F32))
            tmpB1 = eA(nc.sbuf_tensor("tmpB1", [128, T], F32))
            srows_a = eA(nc.sbuf_tensor("srows_a", [1, 2, T], F32))
            H.update(AB=AB1, BB=BB1, sq=sq1, tmpA=tmpA1, tmpB=tmpB1,
                     srows=srows_a, epsA=epsc[:, 0:1])
            xb_marks = emit_norm(c, H, "ln")

        with contextlib.ExitStack() as stBC:
            eBC = stBC.enter_context
            qf = eBC(nc.sbuf_tensor("qf", [128, NSLAB, T], BF16))
            kf = eBC(nc.sbuf_tensor("kf", [128, NSLAB, T], BF16))

            with contextlib.ExitStack() as stB1:
                eB = stB1.enter_context
                ropeC = eB(nc.sbuf_tensor("ropeC_s", [128, T], BF16))
                ropeS = eB(nc.sbuf_tensor("ropeS_s", [128, T], BF16))
                pemat = eB(nc.sbuf_tensor("pemat", [128, 4, 128], BF16))
                wtile = [eB(nc.sbuf_tensor(f"wqk{i}", [128, 8, 128], BF16))
                         for i in range(8)]
                rt = [eB(nc.sbuf_tensor(f"rt{i}", [128, 512], F32))
                      for i in range(2)]
                eo_rope = [eB(nc.sbuf_tensor(f"eo{i}", [128, 512], BF16))
                           for i in range(4)]

                @block.sync
                def _(sy):
                    c.dma_guard(sy, "S")
                    c.inc("dL", sy.dma_start(out=ropeC[:], in_=ropeC_d), 16)
                    c.inc("dL", sy.dma_start(out=ropeS[:], in_=ropeS_d), 16)
                    c.inc("dL", sy.dma_start(out=pemat[:], in_=pe_d.rearrange(
                        "m p q -> p m q")), 16)
                ld_rope = c.n("dL")

                def oc_of(st, s):
                    return (0, 4, 8, 12)[st] + s

                wq_marks = {}
                wq_free = {}
                rope_marks = {}
                mm_marks = {}
                rp_marks = {}
                rpe_marks = {}
                prev_eo = {0: None, 1: None}
                prev_bank = {0: None, 1: None}

                @block.gpsimd
                def _(g):
                    if stage < 2:
                        return
                    c.dma_guard(g, "G")
                    for s in range(2):
                        for st in range(4):
                            wq_marks[(s, st)] = c.inc("dW", g.dma_start(
                                out=wtile[(s % 2) * 4 + st][:],
                                in_=wqkv_qk_d[:, sl(oc_of(st, s), 128)]
                                .rearrange("(kc p) m -> p kc m", p=128)), 16)

                keys = []
                for s in range(4 if stage >= 2 else 0):
                    for pair in ("q", "k"):
                        keys.append((s, pair))

                def emit_wtile_load(s):
                    @block.gpsimd
                    def _(g, s=s):
                        for st in range(4):
                            c.w(g, "G", "tS", wq_free[(s - 2, st)])
                            wq_marks[(s, st)] = c.inc("dW", g.dma_start(
                                out=wtile[(s % 2) * 4 + st][:],
                                in_=wqkv_qk_d[:, sl(oc_of(st, s), 128)]
                                .rearrange("(kc p) m -> p kc m", p=128)), 16)

                def emit_qkv_mm(s, pair, tc):
                    key = (s, pair, tc)
                    par = tc % 2
                    st_e, st_o = (0, 1) if pair == "q" else (2, 3)
                    pe_bank, po_bank = PS[par], PS[2 + par]

                    @block.tensor
                    def _(t):
                        _wait_xb(c, t, "T", xb_marks)
                        c.w(t, "T", "dW", wq_marks[(s, st_e)])
                        c.w(t, "T", "dW", wq_marks[(s, st_o)])
                        pk = prev_bank[par]
                        if pk is not None:
                            c.w(t, "T", "vS", rope_marks[pk])
                        for kc in range(NSLAB):
                            ie = t.matmul(
                                pe_bank[:], wtile[(s % 2) * 4 + st_e][:, kc, :],
                                XB[:, kc, sl(tc, 512)],
                                start=(kc == 0), stop=(kc == NSLAB - 1))
                        mm_marks[key + ("e",)] = c.inc("tS", ie)
                        for kc in range(NSLAB):
                            io = t.matmul(
                                po_bank[:], wtile[(s % 2) * 4 + st_o][:, kc, :],
                                XB[:, kc, sl(tc, 512)],
                                start=(kc == 0), stop=(kc == NSLAB - 1))
                        mm_marks[key + ("o",)] = c.inc("tS", io)
                        if tc == NTC - 1:
                            wq_free[(s, st_e)] = c.n("tS")
                            wq_free[(s, st_o)] = c.n("tS")
                        prev_bank[par] = key

                def emit_rope(s, pair, tc):
                    key = (s, pair, tc)
                    par = tc % 2
                    st_e, st_o = (0, 1) if pair == "q" else (2, 3)
                    pe_bank, po_bank = PS[par], PS[2 + par]

                    @block.vector
                    def _(v):
                        c.w(v, "V", "tS", mm_marks[key + ("o",)])
                        c.w(v, "V", "dL", ld_rope)
                        pk = prev_eo[par]
                        if pk is not None:
                            c.w(v, "V", "tS", rp_marks[pk + ("rp",)])
                        be = qk_bias[:, oc_of(st_e, s):oc_of(st_e, s) + 1]
                        bo = qk_bias[:, oc_of(st_o, s):oc_of(st_o, s) + 1]
                        Cs = ropeC[:, sl(tc, 512)]
                        Ss = ropeS[:, sl(tc, 512)]
                        v.scalar_tensor_tensor(rt[0][:], pe_bank[:], be, Cs,
                                               ALU.add, ALU.mult)
                        v.scalar_tensor_tensor(rt[1][:], po_bank[:], bo, Ss,
                                               ALU.add, ALU.mult)
                        v.tensor_tensor(eo_rope[par * 2][:], rt[0][:],
                                        rt[1][:], ALU.subtract)
                        v.scalar_tensor_tensor(rt[0][:], po_bank[:], bo, Cs,
                                               ALU.add, ALU.mult)
                        v.scalar_tensor_tensor(rt[1][:], pe_bank[:], be, Ss,
                                               ALU.add, ALU.mult)
                        rope_marks[key] = c.inc("vS", v.tensor_tensor(
                            eo_rope[par * 2 + 1][:], rt[0][:],
                            rt[1][:], ALU.add))
                        prev_eo[par] = key

                def emit_repack(s, pair, tc):
                    key = (s, pair, tc)
                    par = tc % 2
                    dstf = qf if pair == "q" else kf

                    @block.tensor
                    def _(t):
                        c.w(t, "T", "vS", rope_marks[key])
                        c.w(t, "T", "dL", ld_rope)
                        for u2 in range(2):
                            bank = PS[4 + u2]
                            pk = ("rpe", 4 + u2)
                            if pk in rpe_marks:
                                c.w(t, "T", "vS", rpe_marks[pk])
                            t.matmul(bank[:], pemat[:, u2, :],
                                     eo_rope[par * 2][:],
                                     start=True, stop=False)
                            ii = t.matmul(bank[:], pemat[:, 2 + u2, :],
                                          eo_rope[par * 2 + 1][:],
                                          start=False, stop=True)
                            rp_marks[key + ("rp", u2)] = c.inc("tS", ii)
                        rp_marks[key + ("rp",)] = c.n("tS")

                    @block.vector
                    def _(v):
                        for u2 in range(2):
                            f = 2 * s + u2
                            c.w(v, "V", "tS", rp_marks[key + ("rp", u2)])
                            m = c.inc("vS", v.tensor_copy(
                                dstf[:, f, sl(tc, 512)], PS[4 + u2][:]))
                            rpe_marks[("rpe", 4 + u2)] = m

                flat = [(s, pair, tc) for (s, pair) in keys
                        for tc in range(NTC)]
                for ki, (s, pair, tc) in enumerate(flat):
                    if tc == 0 and pair == "q" and s >= 2:
                        emit_wtile_load(s)
                    emit_qkv_mm(s, pair, tc)
                    emit_rope(s, pair, tc)
                    if ki >= 1:
                        emit_repack(*flat[ki - 1])
                if flat:
                    emit_repack(*flat[-1])

            with contextlib.ExitStack() as stB2:
                eB2 = stB2.enter_context
                wv_sb = eB2(nc.sbuf_tensor("wv_sb", [128, 8, D], BF16))
                vbias_sb = eB2(nc.sbuf_tensor("vbias_sb", [128, D], F32))

                @block.sync
                def _(sy):
                    c.dma_guard(sy, "S")
                    c.inc("dL", sy.dma_start(out=vbias_sb[:], in_=v_bias_d),
                          16)
                ld_vbias = c.n("dL")

                @block.gpsimd
                def _(g):
                    c.dma_guard(g, "G")
                    wq_marks["wv"] = c.inc("dW", g.dma_start(
                        out=wv_sb[:], in_=wqkv_v_d.rearrange(
                            "(kc p) m -> p kc m", p=128)), 16)

                v_mm_marks = {}
                v_ev_marks = {}
                for tb in range(NTB if stage >= 3 else 0):
                    @block.tensor
                    def _(t, tb=tb):
                        c.w(t, "T", "dW", wq_marks["wv"])
                        for half in range(2):
                            gi = tb * 2 + half
                            bank = PS[6 + gi % 2]
                            if gi >= 2:
                                c.w(t, "T", "vS", v_ev_marks[gi - 2])
                            for kc in range(NSLAB):
                                ii = t.matmul(bank[:],
                                              XB[:, kc, sl(tb, 128)],
                                              wv_sb[:, kc, sl(half, 512)],
                                              start=(kc == 0),
                                              stop=(kc == NSLAB - 1))
                            v_mm_marks[gi] = c.inc("tS", ii)

                    @block.vector
                    def _(v, tb=tb):
                        c.w(v, "V", "dL", ld_vbias)
                        c.w(v, "V", "tS", v_mm_marks[tb * 2 + 1])
                        for half in range(2):
                            gi = tb * 2 + half
                            bank = PS[6 + gi % 2]
                            dst = XB[:, 4 * half:4 * half + 4, sl(tb, 128)]
                            v_ev_marks[gi] = c.inc("vS", v.tensor_tensor(
                                dst,
                                bank[:].rearrange("p (a r) -> p a r", r=128),
                                vbias_sb[:, sl(half, 512)].rearrange(
                                    "p (a r) -> p a r", r=128),
                                ALU.add))

            c.barrier()

            with contextlib.ExitStack() as stC:
                eC = stC.enter_context
                wo_sb = eC(nc.sbuf_tensor("wo_sb", [128, 8, D], BF16))
                ao = [eC(nc.sbuf_tensor(f"ao{i}", [128, 8, 128], BF16))
                      for i in range(2)]
                p_bf = [eC(nc.sbuf_tensor(f"pbf{i}", [128, 384], BF16))
                        for i in range(8)]
                pt_sb = [eC(nc.sbuf_tensor(f"ptb{i}", [128, 384], BF16))
                         for i in range(8)]
                mx = [eC(nc.sbuf_tensor(f"mx{i}", [128, 1], F32))
                      for i in range(8)]
                nb = [eC(nc.sbuf_tensor(f"nb{i}", [128, 1], F32))
                      for i in range(8)]
                rs = [eC(nc.sbuf_tensor(f"rs{i}", [128, 1], F32))
                      for i in range(8)]
                rinv = [eC(nc.sbuf_tensor(f"ri{i}", [128, 1], F32))
                        for i in range(8)]

                wo_mark = {}

                @block.gpsimd
                def _(g):
                    c.dma_guard(g, "G")
                    wo_mark["ld"] = c.inc("dW", g.dma_start(
                        out=wo_sb[:],
                        in_=wo_d.rearrange("(kc p) m -> p kc m", p=128)), 16)

                NU = _NQB * HEADS
                um = {}
                SCB = (0, 1, 4, 5)
                ptv = [PS[2][:].bitcast(BF16), PS[3][:].bitcast(BF16)]

                def pt_slot(u):
                    b = ptv[u % 2]
                    off = 512 * ((u // 2) % 2)
                    return b[:, off:off + 384]

                def kbs_of(qb):
                    lo, hi = max(qb - 1, 0), min(qb + 1, NTB - 1)
                    return list(range(lo, hi + 1)), (128 if qb == 0 else 0)

                def vslice(kb, h):
                    s_idx = 4 * (h // 8) + (h % 8) // 2
                    off = kb * 128 + 64 * (h % 2)
                    return XB[:, s_idx, off:off + 64]

                for u in range(NU + 6):
                    @block.tensor
                    def _(t, u=u):
                        if u < NU:
                            qb, h = divmod(u, HEADS)
                            kbs, moff = kbs_of(qb)
                            W = len(kbs) * 128
                            f, u2 = h // 2, h % 2
                            if u >= 4:
                                c.w(t, "T", "cS", um[("exp", u - 4)])
                                c.w(t, "T", "vS", um[("aoev", u - 4)])
                            t.matmul(
                                PS[SCB[u % 4]][:, 0:W],
                                qf[64 * u2:64 * u2 + 64, f, sl(qb, 128)],
                                kf[64 * u2:64 * u2 + 64, f,
                                   kbs[0] * 128:kbs[0] * 128 + W],
                                start=True, stop=False)
                            ii = t.matmul(
                                PS[SCB[u % 4]][:, 0:W], id_bf[:],
                                mask_bf[:, moff:moff + W],
                                start=False, stop=True)
                            um[("sc", u)] = c.inc("tS", ii)
                        if 0 <= u - 2 < NU:
                            up = u - 2
                            qb, h = divmod(up, HEADS)
                            kbs, moff = kbs_of(qb)
                            c.w(t, "T", "vS", um[("norm", up)])
                            if up >= 4:
                                c.w(t, "T", um[("ptev_eng", up - 4)],
                                    um[("ptev", up - 4)])
                            dst = pt_slot(up)
                            for ci in range(len(kbs)):
                                ii = t.transpose(
                                    dst[:, ci * 128:(ci + 1) * 128],
                                    p_bf[up % 8][:, ci * 128:(ci + 1) * 128],
                                    id_bf[:])
                            um[("tr", up)] = c.inc("tS", ii)
                        if 0 <= u - 3 < NU:
                            up = u - 3
                            qb, h = divmod(up, HEADS)
                            kbs, moff = kbs_of(qb)
                            c.w(t, "T", um[("ptev_eng", up)],
                                um[("ptev", up)])
                            if up >= 4:
                                c.w(t, "T", "vS", um[("aoev", up - 4)])
                            for ci, kb in enumerate(kbs):
                                ii = t.matmul(
                                    PS[SCB[up % 4]][0:64, 384:512],
                                    vslice(kb, h),
                                    pt_sb[up % 8][:, sl(ci, 128)],
                                    start=(ci == 0),
                                    stop=(ci == len(kbs) - 1))
                            um[("pv", up)] = c.inc("tS", ii)
                        if u >= 4 and (u - 4) % HEADS == HEADS - 1 and \
                                u - 4 < NU and stage >= 5:
                            qb = (u - 4) // HEADS
                            c.w(t, "T", "vS", um[("aoev", u - 4)])
                            c.w(t, "T", "dW", wo_mark["ld"])
                            if qb >= 1:
                                c.w(t, "T", "vS", um[("xstt_all", qb - 1)])
                            for oc in range(8):
                                bank = PS[6 + oc // 4]
                                reg = bank[:, (oc % 4) * 128:
                                           (oc % 4) * 128 + 128]
                                for kc in range(NSLAB):
                                    ii = t.matmul(
                                        reg, wo_sb[:, kc, sl(oc, 128)],
                                        ao[qb % 2][:, kc, :],
                                        start=(kc == 0),
                                        stop=(kc == NSLAB - 1))
                                um[("wo", qb, oc)] = c.inc("tS", ii)
                            um[("wo_all", qb)] = c.n("tS")

                    @block.vector
                    def _(v, u=u):
                        if u < NU:
                            qb, h = divmod(u, HEADS)
                            kbs, moff = kbs_of(qb)
                            W = len(kbs) * 128
                            c.w(v, "V", "tS", um[("sc", u)])
                            rmk = c.inc("vS", v.tensor_reduce(
                                mx[u % 8][:], PS[SCB[u % 4]][:, 0:W],
                                AX.X, ALU.max, negate=True))
                            c.w(v, "V", "vS", rmk)
                            um[("mask", u)] = c.inc("vS", v.tensor_scalar(
                                nb[u % 8][:], mx[u % 8][:], 0.125, None,
                                ALU.mult))

                    @block.scalar
                    def _(sc, u=u):
                        if u < NU:
                            qb, h = divmod(u, HEADS)
                            kbs, moff = kbs_of(qb)
                            W = len(kbs) * 128
                            c.w(sc, "C", "vS", um[("mask", u)])
                            if u >= 8:
                                c.w(sc, "C", "tS", um[("tr", u - 8)])
                                c.w(sc, "C", "vS", um[("recip", u - 8)])
                            um[("exp", u)] = c.inc("cS", sc.activation(
                                p_bf[u % 8][:, 0:W], PS[SCB[u % 4]][:, 0:W],
                                AF.Exp, bias=nb[u % 8][:], scale=0.125,
                                accum_out=rs[u % 8][:]))

                    @block.vector
                    def _(v, u=u):
                        if 0 <= u - 3 < NU:
                            up = u - 3
                            qb, h = divmod(up, HEADS)
                            c.w(v, "V", "tS", um[("pv", up)])
                            if qb >= 2 and stage >= 5:
                                c.w(v, "V", "tS", um[("wo_all", qb - 2)])
                            um[("aoev", up)] = c.inc("vS", v.tensor_copy(
                                ao[qb % 2][64 * (h % 2):64 * (h % 2) + 64,
                                           h // 2, :],
                                PS[SCB[up % 4]][0:64, 384:512]))
                        if 0 <= u - 2 < NU:
                            up = u - 2
                            qb, h = divmod(up, HEADS)
                            kbs, _m = kbs_of(qb)
                            W = len(kbs) * 128
                            c.w(v, "V", "tS", um[("tr", up)])
                            if up >= 8:
                                c.w(v, "V", "tS", um[("pv", up - 8)])
                            um[("ptev", up)] = c.inc("vS", v.tensor_copy(
                                pt_sb[up % 8][:, 0:W], pt_slot(up)[:, 0:W]))
                            um[("ptev_eng", up)] = "vS"
                        if 0 <= u - 1 < NU:
                            up = u - 1
                            qb, h = divmod(up, HEADS)
                            kbs, _m = kbs_of(qb)
                            W = len(kbs) * 128
                            c.w(v, "V", "cS", um[("exp", up)])
                            um[("recip", up)] = c.inc("vS", v.reciprocal(
                                rinv[up % 8][:], rs[up % 8][:]))
                            c.w(v, "V", "vS", um[("recip", up)])
                            um[("norm", up)] = c.inc("vS", v.tensor_scalar(
                                p_bf[up % 8][:, 0:W], p_bf[up % 8][:, 0:W],
                                rinv[up % 8][:], None, ALU.mult))
                        if u >= 5 and (u - 5) % HEADS == HEADS - 1 and \
                                u - 5 < NU and stage >= 5:
                            qb = (u - 5) // HEADS
                            for oc in range(8):
                                bank = PS[6 + oc // 4]
                                reg = bank[:, (oc % 4) * 128:
                                           (oc % 4) * 128 + 128]
                                c.w(v, "V", "tS", um[("wo_all", qb)])
                                ii = v.scalar_tensor_tensor(
                                    X[:, oc, sl(qb, 128)], reg,
                                    wo_bias[:, oc:oc + 1],
                                    X[:, oc, sl(qb, 128)], ALU.add, ALU.add)
                            um[("xstt_all", qb)] = c.inc("vS", ii)

            if dbg:
                @block.sync
                def _(sy):
                    for nm, val in (("tS", c.n("tS")), ("vS", c.n("vS")),
                                    ("cS", c.n("cS")), ("gS", c.n("gS"))):
                        c.w(sy, "S", nm, val)
                    c.inc("dO", sy.dma_start(out=dbg_xb, in_=XB[:]), 16)
                    c.inc("dO", sy.dma_start(out=dbg_qf, in_=qf[:]), 16)
                    c.inc("dO", sy.dma_start(out=dbg_kf, in_=kf[:]), 16)
                    if stage >= 4:
                        c.inc("dO", sy.dma_start(
                            out=dbg_ao, in_=ao[0][:]), 16)
                        c.inc("dO", sy.dma_start(
                            out=dbg_pt, in_=pt_sb[0][:]), 16)
                        c.inc("dO", sy.dma_start(
                            out=dbg_pb, in_=p_bf[0][:]), 16)
                        c.inc("dO", sy.dma_start(
                            out=dbg_sc, in_=sc_sb[0][:]), 16)
                        for di_, src_ in enumerate((mx, nb, rs, rinv)):
                            c.inc("dO", sy.dma_start(
                                out=dbg_s4[:, di_:di_ + 1], in_=src_[0][:],
                                allow_slow_non_contiguous=True), 16)
                    sy.wait_ge(c.sem["dO"][0], c.n("dO"))

            c.barrier()

        if upto >= 2:
            emit_ffn(c, H, 0, dict(gpT=gpT_d, upT=upT_d, dnT=dnT_d),
                     cw_dram, gw_sb, gbias_sb, epsv_sb, epsg_sb, id_f32,
                     gw_mark[0])
            c.barrier()
        if upto >= 3:
            emit_attnb(c, H, dict(wmT=wmT_d, wpT=wpT_d, v0=v0_d, v1=v1_d),
                       os_t)
            c.barrier()
        if upto >= 4:
            emit_ffn(c, H, 1, dict(gpT=gpT_d, upT=upT_d, dnT=dnT_d),
                     cw_dram, gw_sb, gbias_sb, epsv_sb, epsg_sb, id_f32,
                     gw_mark[0])
            c.barrier()

        out_v = out_d.rearrange("(s p) t -> p s t", p=128)

        @block.scalar
        def _(sc):
            c.inc("dO", sc.dma_start(out=out_v[:, 3:6, :],
                                     in_=X[:, 3:6, :]), 16)

        @block.gpsimd
        def _(g):
            c.inc("dO", g.dma_start(out=out_v[:, 6:8, :],
                                    in_=X[:, 6:8, :]), 16)

        @block.sync
        def _(sy):
            c.inc("dO", sy.dma_start(
                out=out_v[:, 0:3, :], in_=X[:, 0:3, :]), 16)
            sy.wait_ge(c.sem["dO"][0], c.n("dO"))

    return nc


def emit_ffn(c, H, fidx, wd_dram, cw_dram, gw_sb, gbias_sb, epsv_sb, epsg_sb,
             id_f32, gw_mark):
    nc, bl = c.nc, c.block
    X, XB, PS = H["X"], H["XB"], H["PS"]

    with contextlib.ExitStack() as stD:
        eD = stD.enter_context

        wg = eD(nc.sbuf_tensor(f"wg{fidx}", [128, 8, FI], BF16))
        em = {}
        e0 = 0

        @bl.gpsimd
        def _(g):
            c.dma_guard(g, "G")
            em[("wgld", e0)] = c.inc("dW", g.dma_start(
                out=wg[:], in_=wd_dram["gpT"][e0:e0 + 1].rearrange(
                    "o (kc p) i -> p (o kc) i", p=128)), 16)

        with contextlib.ExitStack() as stLN:
            eL = stLN.enter_context
            AB = eL(nc.sbuf_tensor(f"ABd{fidx}", [128, T], F32))
            BB = eL(nc.sbuf_tensor(f"BBd{fidx}", [128, T], F32))
            sq = eL(nc.sbuf_tensor(f"sqd{fidx}", [128, NSLAB, T], BF16))
            tmpA = eL(nc.sbuf_tensor(f"tmpAd{fidx}", [128, T], F32))
            tmpB = eL(nc.sbuf_tensor(f"tmpBd{fidx}", [128, T], F32))
            srows_d = eL(nc.sbuf_tensor(f"srows_d{fidx}", [1, 2, T], F32))
            H2 = dict(H)
            H2.update(AB=AB, BB=BB, sq=sq, tmpA=tmpA, tmpB=tmpB,
                      srows=srows_d, epsA=H["epsc"][:, 0:1])
            xb_marks = emit_norm(c, H2, "ln")

        act = eD(nc.sbuf_tensor(f"act{fidx}", [128, FIC, 1024], BF16))
        wu = eD(nc.sbuf_tensor(f"wu{fidx}", [128, 8, FI], BF16))
        WBt = eD(nc.sbuf_tensor(f"WB{fidx}", [128, T], BF16))
        wrows = eD(nc.sbuf_tensor(f"wr{fidx}", [128, T], BF16))
        silu = [eD(nc.sbuf_tensor(f"sil{fidx}_{i}", [128, 512], BF16))
                for i in range(4)]
        uw = [eD(nc.sbuf_tensor(f"uw{fidx}_{i}", [128, 512], BF16))
              for i in range(4)]
        dtmp_box = []
        wd_box = []
        wrow = [wrows[0:1, :], wrows[64:65, :]]

        @bl.gpsimd
        def _(g):
            c.dma_guard(g, "G")
            em[("wuld", e0)] = c.inc("dW", g.dma_start(
                out=wu[:], in_=wd_dram["upT"][e0:e0 + 1].rearrange(
                    "o (kc p) i -> p (o kc) i", p=128)), 16)

        gm = {}
        stG = contextlib.ExitStack()
        eG = stG.enter_context
        lg = eG(nc.sbuf_tensor(f"lg{fidx}", [128, 16, 16], F32))
        cw = eG(nc.sbuf_tensor(f"cw{fidx}", [128, 16, 16], F32))
        gsc = eG(nc.sbuf_tensor(f"gsc{fidx}", [128, 16, 16], F32))
        sfc = eG(nc.sbuf_tensor(f"sfc{fidx}", [128, 16, 16], F32))
        gtmp = eG(nc.sbuf_tensor(f"gtmp{fidx}", [128, 16, 16], F32))
        gs = eG(nc.sbuf_tensor(f"gs{fidx}", [128, 16, 4], F32))
        gt2 = eG(nc.sbuf_tensor(f"gt2{fidx}", [128, 16, 4], F32))
        selg = eG(nc.sbuf_tensor(f"selg{fidx}", [128, 16, 4], F32))
        sele = eG(nc.sbuf_tensor(f"sele{fidx}", [128, 16, 16], F32))
        red = eG(nc.sbuf_tensor(f"red{fidx}", [128, 16], F32))
        red2 = eG(nc.sbuf_tensor(f"red2{fidx}", [128, 16], F32))
        cwT = eG(nc.sbuf_tensor(f"cwT{fidx}", [16, 16, 128], F32))

        @bl.tensor
        def _(t):
            _wait_xb(c, t, "T", xb_marks)
            c.w(t, "T", "dW", gw_mark)
            for tt in range(NTB):
                for kc in range(NSLAB):
                    ii = t.matmul(PS[4][:, sl(tt, 16)],
                                  XB[:, kc, sl(tt, 128)],
                                  gw_sb[:, kc, :],
                                  start=(kc == 0), stop=(kc == NSLAB - 1))
            gm["lg"] = c.inc("tS", ii)

        @bl.vector
        def _(v):
            c.w(v, "V", "tS", gm["lg"])
            gm["lgev"] = c.inc("vS", v.tensor_copy(
                lg[:].rearrange("p a b -> p (a b)"), PS[4][:, 0:256]))

        @bl.scalar
        def _(sc):
            c.w(sc, "C", "vS", gm["lgev"])
            gm["sig"] = c.inc("cS", sc.activation(gsc[:], lg[:], AF.Sigmoid))

        @bl.vector
        def _(v):
            c.w(v, "V", "cS", gm["sig"])
            v.tensor_tensor(sfc[:], gsc[:], gbias_sb[:], ALU.add)
            g4 = sfc[:].rearrange("p a (g e) -> p a g e", e=4)
            pairs = [(0, 1), (0, 2), (0, 3), (1, 2), (1, 3), (2, 3)]
            i0, j0 = pairs[0]
            v.tensor_tensor(gs[:], g4[:, :, :, i0], g4[:, :, :, j0], ALU.add)
            for (i, j) in pairs[1:]:
                v.tensor_tensor(gt2[:], g4[:, :, :, i], g4[:, :, :, j],
                                ALU.add)
                v.tensor_tensor(gs[:], gs[:], gt2[:], ALU.max)
            v.tensor_tensor(gs[:], gs[:], epsg_sb[:], ALU.add)
            v.memset(selg[:], 0.0)
            for _it in range(2):
                c.w(v, "V", "vS", c.inc("vS", v.tensor_reduce(
                    red[:], gs[:], AX.X, ALU.max)))
                mxb = red[:].unsqueeze(2).to_broadcast([128, 16, 4])
                v.tensor_tensor(gt2[:], gs[:], mxb, ALU.is_equal)
                v.tensor_tensor(selg[:], selg[:], gt2[:], ALU.add)
                v.scalar_tensor_tensor(gs[:], gt2[:], NEG, gs[:],
                                       ALU.mult, ALU.add)
            sm = selg[:].unsqueeze(3).to_broadcast([128, 16, 4, 4])
            v.tensor_tensor(gtmp[:].rearrange("p a (g e) -> p a g e", e=4),
                            sfc[:].rearrange("p a (g e) -> p a g e", e=4),
                            sm, ALU.mult)
            v.tensor_tensor(gtmp[:], gtmp[:], epsv_sb[:], ALU.add)
            v.memset(sele[:], 0.0)
            for _it in range(4):
                c.w(v, "V", "vS", c.inc("vS", v.tensor_reduce(
                    red[:], gtmp[:], AX.X, ALU.max)))
                mxb = red[:].unsqueeze(2).to_broadcast([128, 16, 16])
                v.tensor_tensor(sfc[:], gtmp[:], mxb, ALU.is_equal)
                v.tensor_tensor(sele[:], sele[:], sfc[:], ALU.add)
                v.scalar_tensor_tensor(gtmp[:], sfc[:], NEG, gtmp[:],
                                       ALU.mult, ALU.add)
            v.tensor_tensor(cw[:], gsc[:], sele[:], ALU.mult)
            c.w(v, "V", "vS", c.inc("vS", v.tensor_reduce(
                red[:], cw[:], AX.X, ALU.add)))
            v.tensor_scalar(red[:], red[:], 1e-20, None, ALU.add)
            c.w(v, "V", "vS", c.inc("vS", v.reciprocal(red2[:], red[:])))
            gm["cw"] = c.inc("vS", v.tensor_tensor(
                cw[:], cw[:],
                red2[:].unsqueeze(2).to_broadcast([128, 16, 16]),
                ALU.mult))

        trm = {}

        def emit_cw_store():
            for tt in range(NTB):
                bank = PS[5 + tt % 2][:]

                @bl.tensor
                def _(t, tt=tt, bank=bank):
                    c.w(t, "T", "vS", gm["cw"])
                    if tt >= 2:
                        c.w(t, "T", "vS", trm[("ev", tt - 2)])
                    trm[("tr", tt)] = c.inc("tS", t.transpose(
                        bank[0:16, 0:128], cw[:, tt, :], id_f32[:]))

                @bl.vector
                def _(v, tt=tt, bank=bank):
                    c.w(v, "V", "tS", trm[("tr", tt)])
                    trm[("ev", tt)] = c.inc("vS", v.tensor_copy(
                        cwT[:, tt, :], bank[0:16, 0:128]))

            @bl.sync
            def _(sy):
                c.w(sy, "S", "vS", trm[("ev", NTB - 1)])
                gm["cwst"] = c.inc("dC", sy.dma_start(
                    out=cw_dram, in_=cwT[:].rearrange("p a b -> p (a b)")),
                    16)
                if H.get("dbg_cw") is not None and fidx == 0:
                    c.inc("dC", sy.dma_start(
                        out=H["dbg_cw"], in_=cwT[:].rearrange(
                            "p a b -> p (a b)")), 16)
            stG.close()

        with contextlib.ExitStack() as stE:
            def dma_w(g, ei, prev=None):
                if prev is not None:
                    c.w(g, "G", "tS", em[("gmm_all", prev)])
                em[("wgld", ei)] = c.inc("dW", g.dma_start(
                    out=wg[:], in_=wd_dram["gpT"][ei:ei + 1].rearrange(
                        "o (kc p) i -> p (o kc) i", p=128)), 16)
                if prev is not None:
                    c.w(g, "G", "tS", em[("umm_all", prev)])
                em[("wuld", ei)] = c.inc("dW", g.dma_start(
                    out=wu[:], in_=wd_dram["upT"][ei:ei + 1].rearrange(
                        "o (kc p) i -> p (o kc) i", p=128)), 16)
                if prev is not None:
                    c.w(g, "G", "tS", em[("dmm_all", prev)])
                em[("wdld", ei)] = c.inc("dW", g.dma_start(
                    out=wd_box[0][:],
                    in_=wd_dram["dnT"][ei:ei + 1].rearrange(
                        "o (kc p) h -> p (o kc) h", p=128)), 16)

            esel = list(range(NE)) if _ESEL is None else list(_ESEL)

            def dma_w_sel(g, pos):
                if pos < len(esel):
                    dma_w(g, esel[pos])

            def emit_wrow(ei):
                par = ei % 2

                @bl.gpsimd
                def _(g, ei=ei, par=par):
                    c.dma_guard(g, "G")
                    c.w(g, "G", "dC", gm["cwst"])
                    if ("wbev", ei - 2) in em:
                        c.w(g, "G", "vS", em[("wbev", ei - 2)])
                    em[("wrow", ei)] = c.inc("dW", g.dma_start(
                        out=wrow[par], in_=cw_dram[ei:ei + 1, :]), 16)

            def emit_wb(ei):
                par = ei % 2
                for ch in range(NTC):
                    @bl.tensor
                    def _(t, ei=ei, ch=ch, par=par):
                        c.w(t, "T", "dW", em[("wrow", ei)])
                        bank = PS[6 + ch % 2]
                        pk = ("wbbank", ch % 2)
                        if pk in em:
                            c.w(t, "T", "vS", em[pk])
                        elif ("ev", NTB - 1) in trm:
                            c.w(t, "T", "vS", trm[("ev", NTB - 1)])
                        em[("wbmm", ei, ch)] = c.inc("tS", t.matmul(
                            bank[:],
                            H["ones_row_bf"][0:1, :] if par == 0 else
                            H["ones_row_bf"][64:65, :],
                            wrow[par][:, sl(ch, 512)],
                            start=True, stop=True))

                    @bl.vector
                    def _(v, ei=ei, ch=ch, par=par):
                        c.w(v, "V", "tS", em[("wbmm", ei, ch)])
                        bank = PS[6 + ch % 2]
                        em[("wbbank", ch % 2)] = c.inc(
                            "vS", v.tensor_copy(WBt[:, sl(ch, 512)],
                                                bank[:]))
                        if ch == NTC - 1:
                            em[("wbev", ei)] = em[("wbbank", ch % 2)]

            for epos, ei in enumerate(esel):
                routed = ei < 16
                nxt = esel[epos + 1] if epos + 1 < len(esel) else None
                for th in range(2):
                    if th == 0 and epos > 0 and nxt is not None and nxt < 16:
                        emit_wrow(nxt)
                    raw0 = (epos == 0 and th == 0 and routed)
                    for i in range(FIC):
                        for tc2 in range(2):
                            tcg = th * 2 + tc2
                            gi = i * 2 + tc2
                            gbank = PS[gi % 2]
                            ubank = PS[2 + gi % 2]

                            @bl.tensor
                            def _(t, ei=ei, th=th, i=i, tc2=tc2, tcg=tcg,
                                  gi=gi, gbank=gbank, ubank=ubank):
                                c.w(t, "T", "dW", em[("wgld", ei)])
                                _wait_xb(c, t, "T", xb_marks)
                                pk = ("gprev", gi % 2)
                                if pk in em:
                                    c.w(t, "T", em[pk][0], em[pk][1])
                                for kc in range(NSLAB):
                                    ii = t.matmul(
                                        gbank[:], wg[:, kc, sl(i, 128)],
                                        XB[:, kc, sl(tcg, 512)],
                                        start=(kc == 0),
                                        stop=(kc == NSLAB - 1))
                                em[("gmm", ei, th, i, tc2)] = c.inc("tS", ii)
                                c.w(t, "T", "dW", em[("wuld", ei)])
                                pk = ("uprev", gi % 2)
                                if pk in em:
                                    c.w(t, "T", em[pk][0], em[pk][1])
                                for kc in range(NSLAB):
                                    ii = t.matmul(
                                        ubank[:], wu[:, kc, sl(i, 128)],
                                        XB[:, kc, sl(tcg, 512)],
                                        start=(kc == 0),
                                        stop=(kc == NSLAB - 1))
                                em[("umm", ei, th, i, tc2)] = c.inc("tS", ii)

                            @bl.scalar
                            def _(sc, ei=ei, th=th, i=i, tc2=tc2, gi=gi,
                                  gbank=gbank):
                                c.w(sc, "C", "tS", em[("gmm", ei, th, i,
                                                       tc2)])
                                j = gi % 4
                                pk = ("silu_used", j)
                                if pk in em:
                                    c.w(sc, "C", "gS", em[pk])
                                em[("silu", ei, th, i, tc2)] = c.inc(
                                    "cS", sc.activation(
                                        silu[j][:], gbank[:], AF.Silu))
                                em[("gprev", gi % 2)] = \
                                    ("cS", em[("silu", ei, th, i, tc2)])

                            if raw0:
                                @bl.scalar
                                def _(sc, ei=ei, th=th, i=i, tc2=tc2, gi=gi,
                                      ubank=ubank):
                                    c.w(sc, "C", "tS", em[("umm", ei, th, i,
                                                           tc2)])
                                    j = gi % 4
                                    pk = ("uw_used", j)
                                    if pk in em:
                                        c.w(sc, "C", "gS", em[pk])
                                    ii = sc.copy(uw[j][:], ubank[:])
                                    em[("uwv", ei, th, i, tc2)] = \
                                        c.inc("cS", ii)
                                    em[("uwv_eng", ei, th, i, tc2)] = "cS"
                                    em[("uprev", gi % 2)] = \
                                        ("cS", em[("uwv", ei, th, i, tc2)])
                            else:
                                @bl.vector
                                def _(v, ei=ei, th=th, i=i, tc2=tc2, tcg=tcg,
                                      gi=gi, ubank=ubank, routed=routed):
                                    c.w(v, "V", "tS", em[("umm", ei, th, i,
                                                          tc2)])
                                    j = gi % 4
                                    pk = ("uw_used", j)
                                    if pk in em:
                                        c.w(v, "V", "gS", em[pk])
                                    if routed:
                                        c.w(v, "V", "vS", em[("wbev", ei)])
                                        ii = v.tensor_tensor(
                                            uw[j][:], ubank[:],
                                            WBt[:, sl(tcg, 512)], ALU.mult)
                                    else:
                                        ii = v.tensor_copy(uw[j][:],
                                                           ubank[:])
                                    em[("uwv", ei, th, i, tc2)] = \
                                        c.inc("vS", ii)
                                    em[("uwv_eng", ei, th, i, tc2)] = "vS"
                                    em[("uprev", gi % 2)] = \
                                        ("vS", em[("uwv", ei, th, i, tc2)])

                            @bl.gpsimd
                            def _(g, ei=ei, th=th, i=i, tc2=tc2, gi=gi):
                                j = gi % 4
                                c.w(g, "G", "cS", em[("silu", ei, th, i,
                                                      tc2)])
                                c.w(g, "G", em[("uwv_eng", ei, th, i, tc2)],
                                    em[("uwv", ei, th, i,
                                        tc2)])
                                pk = ("actused",)
                                if pk in em:
                                    c.w(g, "G", "tS", em[pk])
                                ii = g.tensor_tensor(
                                    act[:, i, sl(tc2, 512)], silu[j][:],
                                    uw[j][:], ALU.mult)
                                mk = c.inc("gS", ii)
                                em[("act", ei, th, i, tc2)] = mk
                                em[("silu_used", j)] = mk
                                em[("uw_used", j)] = mk

                    if raw0:
                        emit_cw_store()
                        wd_box.append(eD(nc.sbuf_tensor(
                            f"wd{fidx}", [128, FIC, D], BF16)))
                        dtmp_box.append(eD(nc.sbuf_tensor(
                            f"dtmp{fidx}", [128, 512], BF16)))

                        @bl.gpsimd
                        def _(g, ei=ei):
                            c.dma_guard(g, "G")
                            em[("wdld", ei)] = c.inc("dW", g.dma_start(
                                out=wd_box[0][:],
                                in_=wd_dram["dnT"][ei:ei + 1].rearrange(
                                    "o (kc p) h -> p (o kc) h", p=128)), 16)
                        emit_wrow(ei)
                        emit_wb(ei)
                        if nxt is not None and nxt < 16:
                            emit_wrow(nxt)
                    if th == 1 and nxt is not None and nxt < 16:
                        emit_wb(nxt)

                    for h in range(8):
                        for tc2 in range(2):
                            tcg = th * 2 + tc2
                            gi = h * 2 + tc2
                            bank = PS[4 + gi % 2]

                            @bl.tensor
                            def _(t, ei=ei, th=th, h=h, tc2=tc2, tcg=tcg,
                                  gi=gi, bank=bank, raw0=raw0):
                                c.w(t, "T", "dW", em[("wdld", ei)])
                                if raw0 and h == 0 and tc2 == 0:
                                    c.w(t, "T", "vS", trm[("ev", NTB - 1)])
                                pk = ("dprev", gi % 2)
                                if pk in em:
                                    c.w(t, "T", "vS", em[pk])
                                for i in range(FIC):
                                    c.w(t, "T", "gS",
                                        em[("act", ei, th, i, tc2)])
                                    ii = t.matmul(
                                        bank[:],
                                        wd_box[0][:, i, sl(h, 128)],
                                        act[:, i, sl(tc2, 512)],
                                        start=(i == 0), stop=(i == FIC - 1))
                                em[("dmm", ei, th, h, tc2)] = c.inc("tS", ii)
                                if h == 7 and tc2 == 1:
                                    em[("actused",)] = c.n("tS")

                            @bl.vector
                            def _(v, ei=ei, th=th, h=h, tc2=tc2, tcg=tcg,
                                  gi=gi, bank=bank, raw0=raw0):
                                c.w(v, "V", "tS", em[("dmm", ei, th, h, tc2)])
                                if raw0:
                                    dtmp = dtmp_box[0]
                                    c.w(v, "V", "vS", em[("wbev", ei)])
                                    v.tensor_tensor(
                                        dtmp[:], bank[:],
                                        WBt[:, sl(tcg, 512)], ALU.mult)
                                    ii = v.tensor_tensor(
                                        X[:, h, sl(tcg, 512)],
                                        X[:, h, sl(tcg, 512)], dtmp[:],
                                        ALU.add)
                                else:
                                    ii = v.tensor_tensor(
                                        X[:, h, sl(tcg, 512)],
                                        X[:, h, sl(tcg, 512)], bank[:],
                                        ALU.add)
                                em[("dprev", gi % 2)] = c.inc("vS", ii)

                    if th == 1 and H.get("dbg_act") is not None and \
                            fidx == 0 and epos == len(esel) - 1:
                        @bl.sync
                        def _(sy, ei=ei):
                            for i2 in range(FIC):
                                for t2 in range(2):
                                    c.w(sy, "S", "gS",
                                        em[("act", ei, 1, i2, t2)])
                            c.inc("dO", sy.dma_start(out=H["dbg_act"],
                                                     in_=act[:]), 16)
                            sy.wait_ge(c.sem["dO"][0], c.n("dO"))
                    if th == 1:
                        em[("gmm_all", ei)] = em[("gmm", ei, 1, FIC - 1, 1)]
                        em[("umm_all", ei)] = em[("umm", ei, 1, FIC - 1, 1)]
                        em[("dmm_all", ei)] = em[("dmm", ei, 1, 7, 1)]

                        @bl.gpsimd
                        def _(g, ei=ei, epos=epos):
                            if epos + 1 < len(esel):
                                dma_w(g, esel[epos + 1], prev=ei)



def emit_attnb(c, H, wdr, os_t):
    nc, bl = c.nc, c.block
    X, XB, PS = H["X"], H["XB"], H["PS"]
    NQ = 4
    NCH = 32
    L = 16

    with contextlib.ExitStack() as stE:
        eE = stE.enter_context
        AB = eE(nc.sbuf_tensor("ABe", [128, T], F32))
        BB = eE(nc.sbuf_tensor("BBe", [128, T], F32))
        wm_sb = eE(nc.sbuf_tensor("wm_sb", [128, 8, D], BF16))
        wp_sb = eE(nc.sbuf_tensor("wp_sb", [128, 4, D], BF16))
        wa = eE(nc.sbuf_tensor("wa", [128, 2, NCH], F32))
        wb_ = eE(nc.sbuf_tensor("wb_", [128, 2, NCH], F32))
        carry = eE(nc.sbuf_tensor("carry", [128, 2, 2], F32))
        v0_sb = eE(nc.sbuf_tensor("v0_sb", [128, 2], F32))
        v1_sb = eE(nc.sbuf_tensor("v1_sb", [128, 2], F32))
        em = {}

        @bl.sync
        def _(sy):
            c.dma_guard(sy, "S")
            c.inc("dL", sy.dma_start(out=v0_sb[:], in_=wdr["v0"]), 16)
            c.inc("dL", sy.dma_start(out=v1_sb[:], in_=wdr["v1"]), 16)
        ld_v = c.n("dL")

        @bl.gpsimd
        def _(g):
            c.dma_guard(g, "G")
            em["wmld"] = c.inc("dW", g.dma_start(
                out=wm_sb[:], in_=wdr["wmT"].rearrange(
                    "(kc p) m -> p kc m", p=128)), 16)
            em["wpld"] = c.inc("dW", g.dma_start(
                out=wp_sb[:], in_=wdr["wpT"].rearrange(
                    "(kc p) m -> p kc m", p=128)), 16)

        with contextlib.ExitStack() as stR:
            eR = stR.enter_context
            sq = eR(nc.sbuf_tensor("sqe", [128, NSLAB, T], BF16))
            srows_e = eR(nc.sbuf_tensor("srows_e", [1, 2, T], F32))
            tmpAe = eR(nc.sbuf_tensor("tmpAe", [128, T], F32))
            tmpBe = eR(nc.sbuf_tensor("tmpBe", [128, T], F32))
            H2 = dict(H)
            H2.update(AB=AB, BB=BB, sq=sq, tmpA=tmpAe, tmpB=tmpBe,
                      srows=srows_e, epsA=H["epsc"][:, 0:1])
            xn_marks = emit_norm(c, H2, "ln")

        @bl.vector
        def _(v):
            if True:
                c.w(v, "V", "vS", max(xn_marks[:5]))
                c.w(v, "V", "gS", max(xn_marks[5:]))
                c.inc("vS", v.tensor_scalar(AB[:], AB[:], 1.0, None,
                                            ALU.add))
        ab1_mark = c.n("vS")

        M1 = eE(nc.sbuf_tensor("M1", [128, NSLAB, 512], F32))
        M2 = eE(nc.sbuf_tensor("M2", [128, NSLAB, 512], F32))
        n2 = eE(nc.sbuf_tensor("n2", [128, 2, 512], F32))
        n2b = eE(nc.sbuf_tensor("n2b", [128, 2, 512], F32))
        hb0 = eE(nc.sbuf_tensor("hb0", [128, 2, 512], BF16))
        hb1 = eE(nc.sbuf_tensor("hb1", [128, 2, 512], BF16))
        tq0 = eE(nc.sbuf_tensor("tq0", [128, 2, 512], F32))
        tq1 = eE(nc.sbuf_tensor("tq1", [128, 2, 512], F32))

        def plane(buf, p, khslice=slice(None), tslice=slice(None)):
            v4 = buf[:].rearrange("p (a k) t -> p a k t", k=2)
            return v4[:, p, khslice, tslice]

        for q4 in range(NQ):
            for oc in range(8):
                bank = PS[oc % 2]

                @bl.tensor
                def _(t, q4=q4, oc=oc, bank=bank):
                    c.w(t, "T", "dW", em["wmld"])
                    _wait_xb(c, t, "T", xn_marks)
                    if oc >= 2 or q4 > 0:
                        c.w(t, "T", "vS", em.get(("mev", q4, oc - 2),
                                                 em.get(("scan", q4 - 1),
                                                        0)))
                    for kc in range(NSLAB):
                        ii = t.matmul(bank[:], wm_sb[:, kc, sl(oc, 128)],
                                      XB[:, kc, sl(q4, 512)],
                                      start=(kc == 0), stop=(kc == NSLAB - 1))
                    em[("mmm", q4, oc)] = c.inc("tS", ii)

                @bl.vector
                def _(v, q4=q4, oc=oc, bank=bank):
                    c.w(v, "V", "tS", em[("mmm", q4, oc)])
                    if q4 > 0:
                        c.w(v, "V", "vS", em.get(("scan", q4 - 1), 0))
                    em[("mev", q4, oc)] = c.inc("vS", v.tensor_copy(
                        M1[:, oc, :], bank[:]))

            @bl.gpsimd
            def _(g, q4=q4):
                c.w(g, "G", "vS", em[("mev", q4, 7)])
                g.tensor_tensor(n2[:], plane(M1, 0), plane(M1, 0), ALU.mult)
                for p in range(1, 4):
                    g.tensor_tensor(n2b[:], plane(M1, p), plane(M1, p),
                                    ALU.mult)
                    ii = g.tensor_tensor(n2[:], n2[:], n2b[:], ALU.add)
                em[("n2", q4)] = c.inc("gS", ii)

            @bl.scalar
            def _(sc, q4=q4):
                c.w(sc, "C", "gS", em[("n2", q4)])
                lnm = c.inc("cS", sc.activation(n2[:], n2[:], AF.Ln))
                c.w(sc, "C", "cS", lnm)
                em[("inv", q4)] = c.inc("cS", sc.activation(
                    n2[:], n2[:], AF.Exp, scale=-0.5))

            @bl.gpsimd
            def _(g, q4=q4):
                c.w(g, "G", "cS", em[("inv", q4)])
                g.tensor_scalar(n2[:], n2[:], 1e6, None, ALU.min)
                for p in range(4):
                    ii = g.tensor_tensor(plane(M1, p), plane(M1, p), n2[:],
                                         ALU.mult)
                em[("mnorm", q4)] = c.inc("gS", ii)

            def c4(buf, p, l):
                return plane(buf, p).rearrange(
                    "p k (c l) -> p k c l", l=L)[:, :, :, l:l + 1]

            ts0 = tq0[:].rearrange("p k (c l) -> p k c l", l=L)[:, :, :, 0:1]
            ts1 = tq1[:].rearrange("p k (c l) -> p k c l", l=L)[:, :, :, 0:1]

            @bl.vector
            def _(v, q4=q4):
                c.w(v, "V", "gS", em[("mnorm", q4)])
                if q4 > 0:
                    c.w(v, "V", "gS", em[("happ", q4 - 1)])
                for p in range(4):
                    v.tensor_copy(c4(M2, p, 0), c4(M1, p, 0))
                for l in range(1, L):
                    for (o, a, pp, b2, qq) in ((0, 0, 0, 1, 2),
                                               (1, 0, 1, 1, 3),
                                               (2, 2, 0, 3, 2),
                                               (3, 2, 1, 3, 3)):
                        v.tensor_tensor(ts0, c4(M1, a, l), c4(M2, pp, l - 1),
                                        ALU.mult)
                        v.tensor_tensor(ts1, c4(M1, b2, l), c4(M2, qq, l - 1),
                                        ALU.mult)
                        v.tensor_tensor(c4(M2, o, l), ts0, ts1, ALU.add)
                em[("scan", q4)] = c.inc("vS", v.tensor_copy(
                    wa[:, :, 0:1],
                    (v0_sb[:].unsqueeze(2) if q4 == 0 else
                     carry[:, :, 0:1])))

            def pf(p, ch):
                return plane(M2, p).rearrange(
                    "p k (c l) -> p k c l", l=L)[:, :, ch:ch + 1, L - 1:L]

            def wv3(buf, ch):
                return buf[:, :, ch:ch + 1].unsqueeze(3)

            s0 = tq0[:].rearrange("p k (c l) -> p k c l", l=L)[:, :, 0:1, 0:1]
            s1 = tq1[:].rearrange("p k (c l) -> p k c l", l=L)[:, :, 0:1, 0:1]

            @bl.vector
            def _(v, q4=q4):
                c.w(v, "V", "vS", em[("scan", q4)])
                v.tensor_copy(wb_[:, :, 0:1],
                              (v1_sb[:].unsqueeze(2) if q4 == 0 else
                               carry[:, :, 1:2]))
                for ch in range(1, NCH + 1):
                    wdst = ((wa, wb_) if ch < NCH else (None, None))
                    v.tensor_tensor(s0, pf(0, ch - 1), wv3(wa, ch - 1),
                                    ALU.mult)
                    v.tensor_tensor(s1, pf(1, ch - 1), wv3(wb_, ch - 1),
                                    ALU.mult)
                    dst0 = (wv3(wa, ch) if ch < NCH
                            else carry[:, :, 0:1].unsqueeze(3))
                    v.tensor_tensor(dst0, s0, s1, ALU.add)
                    v.tensor_tensor(s0, pf(2, ch - 1), wv3(wa, ch - 1),
                                    ALU.mult)
                    v.tensor_tensor(s1, pf(3, ch - 1), wv3(wb_, ch - 1),
                                    ALU.mult)
                    dst1 = (wv3(wb_, ch) if ch < NCH
                            else carry[:, :, 1:2].unsqueeze(3))
                    ii = v.tensor_tensor(dst1, s0, s1, ALU.add)
                em[("comb", q4)] = c.inc("vS", ii)

            @bl.gpsimd
            def _(g, q4=q4):
                c.w(g, "G", "vS", em[("comb", q4)])
                if q4 >= 1:
                    c.w(g, "G", "tS", em[("pmm", q4 - 1, 7)])

                def m4(p):
                    return plane(M2, p).rearrange("p k (c l) -> p k c l", l=L)

                wab = wa[:].unsqueeze(3).to_broadcast([128, 2, NCH, L])
                wbb = wb_[:].unsqueeze(3).to_broadcast([128, 2, NCH, L])
                t4a = tq0[:].rearrange("p k (c l) -> p k c l", l=L)
                t4b = tq1[:].rearrange("p k (c l) -> p k c l", l=L)
                g.tensor_tensor(t4a, m4(0), wab, ALU.mult)
                g.tensor_tensor(t4b, m4(1), wbb, ALU.mult)
                g.tensor_tensor(hb0[:].rearrange("p k (c l) -> p k c l", l=L),
                                t4a, t4b, ALU.add)
                g.tensor_tensor(t4a, m4(2), wab, ALU.mult)
                g.tensor_tensor(t4b, m4(3), wbb, ALU.mult)
                ii = g.tensor_tensor(
                    hb1[:].rearrange("p k (c l) -> p k c l", l=L),
                    t4a, t4b, ALU.add)
                em[("happ", q4)] = c.inc("gS", ii)

            for oc in range(8):
                bank = PS[4 + oc % 2]

                @bl.tensor
                def _(t, q4=q4, oc=oc, bank=bank):
                    c.w(t, "T", "dW", em["wpld"])
                    c.w(t, "T", "gS", em[("happ", q4)])
                    if oc >= 2 or q4 > 0:
                        c.w(t, "T", "vS", em.get(("pev", q4, oc - 2),
                                                 em.get(("pev", q4 - 1, 7),
                                                        0)))
                    for chn, hbuf in ((0, hb0), (1, hb0), (2, hb1), (3, hb1)):
                        kh = chn % 2
                        ii = t.matmul(bank[:], wp_sb[:, chn, sl(oc, 128)],
                                      hbuf[:, kh, :],
                                      start=(chn == 0), stop=(chn == 3))
                    em[("pmm", q4, oc)] = c.inc("tS", ii)

                @bl.vector
                def _(v, q4=q4, oc=oc, bank=bank):
                    c.w(v, "V", "tS", em[("pmm", q4, oc)])
                    c.w(v, "V", "vS", ab1_mark)
                    v.scalar_tensor_tensor(
                        tq0[:, 0, :], bank[:], os_t[:, oc:oc + 1],
                        BB[:, sl(q4, 512)], ALU.mult, ALU.add)
                    v.tensor_tensor(tq1[:, 0, :], X[:, oc, sl(q4, 512)],
                                    AB[:, sl(q4, 512)], ALU.mult)
                    ii = v.tensor_tensor(X[:, oc, sl(q4, 512)],
                                         tq0[:, 0, :], tq1[:, 0, :], ALU.add)
                    em[("pev", q4, oc)] = c.inc("vS", ii)


def _host_prep(inp):
    f32 = np.float32
    x = np.asarray(inp["L0_attn_a_hidden_states"], f32)
    x_cores = [np.ascontiguousarray(x[b].T) for b in range(8)]

    Wqkv = np.asarray(inp["L0_attn_a_Wqkv_kernel"], f32)
    qperm = np.empty(1024, np.int64)
    for s in range(4):
        for a in range(4):
            h = 4 * s + a
            for j in range(32):
                qperm[s * 128 + a * 32 + j] = h * 64 + 2 * j
                qperm[512 + s * 128 + a * 32 + j] = h * 64 + 2 * j + 1
    perm_qk = np.concatenate([qperm, 1024 + qperm])
    wqkv_qk = np.ascontiguousarray(Wqkv[:, perm_qk])
    wqkv_v = np.ascontiguousarray(Wqkv[:, 2048:])
    bias = np.asarray(inp["L0_attn_a_Wqkv_bias"], f32)
    qk_bias = np.ascontiguousarray(bias[perm_qk].reshape(16, 128).T)
    v_bias = np.ascontiguousarray(np.tile(bias[None, 2048:], (128, 1)))

    rope = np.asarray(inp["L0_attn_a_rope_cache"], f32)
    cosw = np.tile(np.tile(rope[:, :, 0].T, (1, 8)), (4, 1))
    sinw = np.tile(np.tile(rope[:, :, 1].T, (1, 8)), (4, 1))

    pem = np.zeros((4, 128, 128), f32)
    for j in range(32):
        pem[0, j, j] = 1.0
        pem[0, 32 + j, 64 + j] = 1.0
        pem[1, 64 + j, j] = 1.0
        pem[1, 96 + j, 64 + j] = 1.0
        pem[2, j, 32 + j] = 1.0
        pem[2, 32 + j, 96 + j] = 1.0
        pem[3, 64 + j, 32 + j] = 1.0
        pem[3, 96 + j, 96 + j] = 1.0

    wo = np.ascontiguousarray(np.asarray(inp["L0_attn_a_Wo_kernel"], f32))
    wo_bias = np.ascontiguousarray(
        np.asarray(inp["L0_attn_a_Wo_bias"], f32).reshape(8, 128).T)

    ai = np.arange(128)
    m = np.zeros((128, 384), f32)
    m[:, 0:128] = np.where(ai[None, :] >= ai[:, None], 0.0, NEG)
    m[:, 256:384] = np.where(ai[None, :] <= ai[:, None], 0.0, NEG)

    gate_w = np.asarray(inp["L0_ffn_gate_w"], f32)
    gbias = np.tile(np.asarray(inp["L0_ffn_gate_bias"], f32)[None, None, :],
                    (128, 16, 1)).reshape(128, 256)
    epsv = np.tile((np.arange(15, -1, -1, dtype=f32) * 1e-26)[None, None, :],
                   (128, 16, 1)).reshape(128, 256)
    epsg = np.tile((np.arange(3, -1, -1, dtype=f32) * 1e-26)[None, None, :],
                   (128, 16, 1)).reshape(128, 64)
    gp = np.asarray(inp["L0_ffn_gate_proj"], f32)
    upw = np.asarray(inp["L0_ffn_up_proj"], f32)
    dn = np.asarray(inp["L0_ffn_down_proj"], f32)
    sgw = np.asarray(inp["L0_ffn_shared_gate_w"], f32)
    suw = np.asarray(inp["L0_ffn_shared_up_w"], f32)
    sdw = np.asarray(inp["L0_ffn_shared_down_w"], f32)
    gpT = np.ascontiguousarray(np.concatenate(
        [gp.transpose(0, 2, 1), sgw[:1408].T[None], sgw[1408:].T[None]], 0))
    upT = np.ascontiguousarray(np.concatenate(
        [upw.transpose(0, 2, 1), suw[:1408].T[None], suw[1408:].T[None]], 0))
    dnT = np.ascontiguousarray(np.concatenate(
        [dn.transpose(0, 2, 1), sdw[:, :1408].T[None],
         sdw[:, 1408:].T[None]], 0))

    Wm = np.asarray(inp["L0_attn_b_W_m"], f32)
    perm_m = np.empty(1024, np.int64)
    for plane_i in range(4):
        for k in range(256):
            perm_m[plane_i * 256 + k] = 4 * k + plane_i
    wmT = np.ascontiguousarray(Wm.T[:, perm_m])
    vv = np.asarray(inp["L0_attn_b_v_vec"], f32)
    v0t = np.ascontiguousarray(vv[:, 0, 0].reshape(2, 128).T)
    v1t = np.ascontiguousarray(vv[:, 1, 0].reshape(2, 128).T)
    Wp = np.asarray(inp["L0_attn_b_W_proj"], f32)
    perm_p = np.empty(512, np.int64)
    for ch in range(4):
        cc, kh = ch // 2, ch % 2
        for kp in range(128):
            perm_p[ch * 128 + kp] = 2 * (kh * 128 + kp) + cc
    wpT = np.ascontiguousarray(Wp.T[perm_p])
    os_t = np.ascontiguousarray(
        np.asarray(inp["L0_attn_b_out_scale"], f32).reshape(8, 128).T)

    shared = dict(
        wqkv_qk=wqkv_qk, wqkv_v=wqkv_v, qk_bias=qk_bias, v_bias=v_bias,
        ropeC=np.ascontiguousarray(cosw).astype(BF),
        ropeS=np.ascontiguousarray(sinw).astype(BF),
        pemats=pem.astype(BF), wo=wo, wo_bias=wo_bias, mask384=m,
        mask_bf=m.astype(BF),
        id_bf=np.eye(128, dtype=BF), id_f32=np.eye(128, dtype=f32),
        ones_col_bf=np.ones((128, 1), BF),
        ones_row_bf=np.ones((128, 128), BF),
        ones_row_f32=np.ones((128, 128), f32),
        gwT=np.ascontiguousarray(gate_w.T), gbias=gbias, epsv=epsv, epsg=epsg,
        gpT=gpT, upT=upT, dnT=dnT,
        wmT=wmT, v0t=v0t, v1t=v1t, wpT=wpT, os_t=os_t,
        epsc=np.tile(np.array([[EPS_LN, EPS_RMS]], f32), (128, 1)),
    )
    return shared, x_cores


def kernel(**inputs):
    shared, x_cores = _host_prep(inputs)
    nc = build(_UPTO)
    in_maps = []
    for cid in range(8):
        mcore = dict(shared)
        mcore["x"] = x_cores[cid]
        in_maps.append(mcore)
    res = run_bass_kernel_spmd(nc, in_maps, core_ids=list(range(8)))
    out = np.stack([np.asarray(r["out"], np.float32).T for r in res.results],
                   axis=0)
    return out


# version 20 (final)
# speedup vs baseline: 1.0687x; 1.0687x over previous
import contextlib
import numpy as np
import ml_dtypes
import concourse.bass as bass
import concourse.mybir as mybir
from concourse.bass_utils import run_bass_kernel_spmd

F32 = mybir.dt.float32
BF16 = mybir.dt.bfloat16
AF = mybir.ActivationFunctionType
ALU = mybir.AluOpType
AX = mybir.AxisListType
BF = ml_dtypes.bfloat16

D = 1024
T = 2048
NSLAB = 8
NTB = 16
NTC = 4
HEADS = 16
NE = 18
FI = 1408
FIC = 11
EPS_LN = 1e-5
EPS_RMS = 1.1920929e-07
NEG = -1e30
_UPTO = 4
_STAGE = 99
_DBG = False
_SIMPLE_ROPE = False
_NQB = 16
_ESEL = None


def sl(i, n):
    return slice(i * n, (i + 1) * n)


class Ctx:
    def __init__(self, nc, block, stack):
        self.nc = nc
        self.block = block
        self._waited = {}
        self.sem = {}
        for name in ("tS", "vS", "cS", "gS", "dL", "dW", "dC", "dO", "dS"):
            self.sem[name] = [stack.enter_context(nc.semaphore(name)), 0]

    def inc(self, name, instr, k=1):
        s = self.sem[name]
        instr.then_inc(s[0], k)
        s[1] += k
        return s[1]

    def n(self, name):
        return self.sem[name][1]

    def w(self, eng, engid, name, val=None):
        s = self.sem[name]
        if val is None:
            val = s[1]
        if val is None or val <= 0:
            return
        key = (engid, name)
        if self._waited.get(key, 0) >= val:
            return
        eng.wait_ge(s[0], val)
        self._waited[key] = val

    def dma_guard(self, eng, engid):
        for nm in ("tS", "vS", "cS", "gS"):
            self.w(eng, engid, nm)

    def barrier(self, engines=("t", "v", "c", "g", "s")):
        marks = [(nm, self.n(nm)) for nm in ("tS", "vS", "cS", "gS")]
        bl = self.block
        decs = {"t": (bl.tensor, "T"), "v": (bl.vector, "V"),
                "c": (bl.scalar, "C"), "g": (bl.gpsimd, "G"),
                "s": (bl.sync, "S")}
        for ename in engines:
            dec, eid = decs[ename]

            @dec
            def _(eng, eid=eid):
                for nm, val in marks:
                    self.w(eng, eid, nm, val)


def emit_norm(c, H, kind):
    bl = c.block
    X, XB, PS = H["X"], H["XB"], H["PS"]
    AB, BB = H["AB"], H.get("BB")
    sq, tmpA, tmpB = H["sq"], H.get("tmpA"), H.get("tmpB")
    srows = H["srows"]
    onesb, onesf = H["ones_col_bf"], H["ones_row_f32"]
    epsap = H["epsA"]

    copy_marks = [0] * NSLAB
    sq_marks = [0] * NSLAB

    if kind == "ln":
        @bl.vector
        def _(v):
            for s in range(NSLAB):
                copy_marks[s] = c.inc("vS",
                                      v.tensor_copy(XB[:, s, :], X[:, s, :]))

    @bl.scalar
    def _(sc):
        for s in range(NSLAB):
            sq_marks[s] = c.inc(
                "cS", sc.activation(sq[:, s, :], X[:, s, :], AF.Square))

    stats = []
    if kind == "ln":
        stats.append((XB, copy_marks, "vS", 0))
    stats.append((sq, sq_marks, "cS", 1))
    groups = [(sti, ch) for sti in range(len(stats)) for ch in range(NTC)]
    evac_marks = {}
    grp_marks = {}
    for gi, (sti, ch) in enumerate(groups):
        buf, marks, semn, row = stats[sti]
        bank = PS[gi % 2]

        @bl.tensor
        def _(t, ch=ch, gi=gi, buf=buf, marks=marks, semn=semn, bank=bank):
            if gi >= 2:
                c.w(t, "T", "vS", evac_marks[gi - 2])
            for s in range(NSLAB):
                c.w(t, "T", semn, marks[s])
                ii = t.matmul(bank[0:1, :], onesb[:], buf[:, s, sl(ch, 512)],
                              start=(s == 0), stop=(s == NSLAB - 1))
            grp_marks[gi] = c.inc("tS", ii)

        @bl.vector
        def _(v, gi=gi, row=stats[sti][3], ch=ch, bank=bank):
            c.w(v, "V", "tS", grp_marks[gi])
            evac_marks[gi] = c.inc(
                "vS", v.tensor_copy(srows[0:1, row, sl(ch, 512)],
                                    bank[0:1, :]))

    stats_done_t = c.n("tS")
    rows_done = c.n("vS")
    rm = {}

    if kind == "ln":
        bcl = [(0, tmpA), (1, tmpB)]
    else:
        bcl = [(1, AB)]
    bc_marks = {}
    bc_ev = {}
    for ri, (row, dst) in enumerate(bcl):
        for ch in range(NTC):
            gi = ri * NTC + ch
            bank = PS[2 + gi % 2]

            @bl.tensor
            def _(t, row=row, ch=ch, gi=gi, bank=bank):
                c.w(t, "T", "vS", rows_done)
                if gi >= 2:
                    c.w(t, "T", "vS", bc_ev[gi - 2])
                bc_marks[gi] = c.inc("tS", t.matmul(
                    bank[:], onesf[0:1, :], srows[0:1, row, sl(ch, 512)],
                    start=True, stop=True))

            @bl.vector
            def _(v, dst=dst, ch=ch, gi=gi, bank=bank):
                c.w(v, "V", "tS", bc_marks[gi])
                bc_ev[gi] = c.inc("vS", v.tensor_copy(dst[:, sl(ch, 512)],
                                                      bank[:]))
    bc_done = c.n("vS")

    if kind == "ln":
        @bl.vector
        def _(v):
            c.w(v, "V", "vS", bc_done)
            rm["m"] = c.inc("vS", v.tensor_scalar(
                tmpA[:], tmpA[:], 1.0 / D, None, ALU.mult))

        @bl.scalar
        def _(sc):
            c.w(sc, "C", "vS", rm["m"])
            rm["msq"] = c.inc("cS", sc.activation(AB[:], tmpA[:], AF.Square))

        @bl.vector
        def _(v):
            c.w(v, "V", "cS", rm["msq"])
            rm["var"] = c.inc("vS", v.scalar_tensor_tensor(
                tmpB[:], tmpB[:], 1.0 / D, AB[:], ALU.mult, ALU.subtract))

        @bl.scalar
        def _(sc):
            c.w(sc, "C", "vS", rm["var"])
            lnm = c.inc("cS", sc.activation(tmpB[:], tmpB[:], AF.Ln,
                                            bias=epsap))
            c.w(sc, "C", "cS", lnm)
            rm["a"] = c.inc("cS", sc.activation(
                AB[:], tmpB[:], AF.Exp, scale=-0.5))

        @bl.vector
        def _(v):
            c.w(v, "V", "cS", rm["a"])
            rm["b"] = c.inc("vS", v.scalar_tensor_tensor(
                BB[:], tmpA[:], -1.0, AB[:], ALU.mult, ALU.mult))
        ab_done = rm["b"]
    else:
        @bl.vector
        def _(v):
            c.w(v, "V", "vS", bc_done)
            rm["var"] = c.inc("vS", v.tensor_scalar(
                AB[:], AB[:], 1.0 / D, None, ALU.mult))

        @bl.scalar
        def _(sc):
            c.w(sc, "C", "vS", rm["var"])
            lnm = c.inc("cS", sc.activation(AB[:], AB[:], AF.Ln,
                                            bias=epsap))
            c.w(sc, "C", "cS", lnm)
            rm["a"] = c.inc("cS", sc.activation(
                AB[:], AB[:], AF.Exp, scale=-0.5))

        @bl.vector
        def _(v):
            c.w(v, "V", "cS", rm["a"])
            ab_done = None
        ab_done = c.n("cS")
    apply_marks = [0] * NSLAB

    @bl.vector
    def _(v):
        if kind == "ln":
            c.w(v, "V", "vS", ab_done)
        else:
            c.w(v, "V", "cS", ab_done)
        c.w(v, "V", "tS", stats_done_t)
        for s in range(5):
            if kind == "ln":
                v.tensor_tensor(tmpA[:], X[:, s, :], AB[:], ALU.mult)
                apply_marks[s] = c.inc("vS", v.tensor_tensor(
                    XB[:, s, :], tmpA[:], BB[:], ALU.add))
            else:
                apply_marks[s] = c.inc("vS", v.tensor_tensor(
                    XB[:, s, :], X[:, s, :], AB[:], ALU.mult))

    @bl.gpsimd
    def _(g):
        if kind == "ln":
            c.w(g, "G", "vS", ab_done)
        else:
            c.w(g, "G", "cS", ab_done)
        c.w(g, "G", "tS", stats_done_t)
        for s in range(5, NSLAB):
            if kind == "ln":
                gm1 = c.inc("gS", g.tensor_tensor(tmpB[:], X[:, s, :],
                                                  AB[:], ALU.mult))
                c.w(g, "G", "gS", gm1)
                apply_marks[s] = c.inc("gS", g.tensor_tensor(
                    XB[:, s, :], tmpB[:], BB[:], ALU.add))
            else:
                apply_marks[s] = c.inc("gS", g.tensor_tensor(
                    XB[:, s, :], X[:, s, :], AB[:], ALU.mult))

    return apply_marks


def _wait_xb(c, eng, engid, marks):
    c.w(eng, engid, "vS", max(marks[:5]))
    c.w(eng, engid, "gS", max(marks[5:]))


def build(upto=4, stage=None, dbg=None):
    if stage is None:
        stage = _STAGE
    if dbg is None:
        dbg = _DBG
    nc = bass.Bass("TRN2", target_bir_lowering=False)
    dr = {}

    def din(name, shape, dt=F32):
        dr[name] = nc.dram_tensor(name, shape, dt, kind="ExternalInput").ap()
        return dr[name]

    x_d = din("x", [D, T])
    wqkv_qk_d = din("wqkv_qk", [D, 2048])
    wqkv_v_d = din("wqkv_v", [D, D])
    qk_bias_d = din("qk_bias", [128, 16])
    v_bias_d = din("v_bias", [128, D])
    ropeC_d = din("ropeC", [128, T], BF16)
    ropeS_d = din("ropeS", [128, T], BF16)
    pe_d = din("pemats", [4, 128, 128], BF16)
    wo_d = din("wo", [D, D])
    wo_bias_d = din("wo_bias", [128, 8])
    mask_d = din("mask384", [128, 384])
    mask_bf_d = din("mask_bf", [128, 384], BF16)
    id_bf_d = din("id_bf", [128, 128], BF16)
    id_f32_d = din("id_f32", [128, 128])
    ones_col_bf_d = din("ones_col_bf", [128, 1], BF16)
    ones_row_bf_d = din("ones_row_bf", [128, 128], BF16)
    ones_row_f32_d = din("ones_row_f32", [128, 128])
    gwT_d = din("gwT", [D, 16])
    gbias_d = din("gbias", [128, 256])
    epsv_d = din("epsv", [128, 256])
    epsg_d = din("epsg", [128, 64])
    gpT_d = din("gpT", [NE, D, FI])
    upT_d = din("upT", [NE, D, FI])
    dnT_d = din("dnT", [NE, FI, D])
    wmT_d = din("wmT", [D, D])
    v0_d = din("v0t", [128, 2])
    v1_d = din("v1t", [128, 2])
    wpT_d = din("wpT", [512, D])
    os_d = din("os_t", [128, 8])
    epsc_d = din("epsc", [128, 2])

    out_d = nc.dram_tensor("out", [D, T], F32, kind="ExternalOutput").ap()
    if dbg:
        dbg_xb = nc.dram_tensor("dbg_xb", [128, NSLAB, T], BF16,
                                kind="ExternalOutput").ap()
        H_dbg = True
        dbg_qf = nc.dram_tensor("dbg_qf", [128, NSLAB, T], BF16,
                                kind="ExternalOutput").ap()
        dbg_kf = nc.dram_tensor("dbg_kf", [128, NSLAB, T], BF16,
                                kind="ExternalOutput").ap()
        dbg_ao = nc.dram_tensor("dbg_ao", [128, NSLAB, 128], BF16,
                                kind="ExternalOutput").ap()
        dbg_pt = nc.dram_tensor("dbg_pt", [128, 384], BF16,
                                kind="ExternalOutput").ap()
        dbg_pb = nc.dram_tensor("dbg_pb", [128, 384], BF16,
                                kind="ExternalOutput").ap()
        dbg_sc = nc.dram_tensor("dbg_sc", [128, 384], F32,
                                kind="ExternalOutput").ap()
        dbg_s4 = nc.dram_tensor("dbg_s4", [128, 4], F32,
                                kind="ExternalOutput").ap()
        dbg_cw = nc.dram_tensor("dbg_cw", [16, T], F32,
                                kind="ExternalOutput").ap()
        dbg_act = nc.dram_tensor("dbg_act", [128, FIC, 1024], BF16,
                                 kind="ExternalOutput").ap()
    cw_dram = nc.dram_tensor("cw_scratch", [16, T], F32, kind="Internal").ap()

    with contextlib.ExitStack() as stack:
        e = stack.enter_context
        X = e(nc.sbuf_tensor("X", [128, NSLAB, T], F32))
        XB = e(nc.sbuf_tensor("XB", [128, NSLAB, T], BF16))
        PS = [e(nc.psum_tensor(f"PS{i}", [128, 512], F32)) for i in range(8)]
        qk_bias = e(nc.sbuf_tensor("qk_bias_s", [128, 16], F32))
        wo_bias = e(nc.sbuf_tensor("wo_bias_s", [128, 8], F32))
        mask384 = e(nc.sbuf_tensor("mask384_s", [128, 384], F32))
        mask_bf = e(nc.sbuf_tensor("mask_bf_s", [128, 384], BF16))
        id_bf = e(nc.sbuf_tensor("id_bf_s", [128, 128], BF16))
        id_f32 = e(nc.sbuf_tensor("id_f32_s", [128, 128], F32))
        ones_col_bf = e(nc.sbuf_tensor("ones_col_bf_s", [128, 1], BF16))
        ones_row_bf = e(nc.sbuf_tensor("ones_row_bf_s", [128, 128], BF16))
        ones_row_f32 = e(nc.sbuf_tensor("ones_row_f32_s", [128, 128], F32))
        os_t = e(nc.sbuf_tensor("os_t_s", [128, 8], F32))
        epsc = e(nc.sbuf_tensor("epsc_s", [128, 2], F32))
        gw_sb = e(nc.sbuf_tensor("gw_sb", [128, 8, 16], BF16))
        gbias_sb = e(nc.sbuf_tensor("gbias_sb", [128, 16, 16], F32))
        epsv_sb = e(nc.sbuf_tensor("epsv_sb", [128, 16, 16], F32))
        epsg_sb = e(nc.sbuf_tensor("epsg_sb", [128, 16, 4], F32))

        block = e(nc.Block())
        c = Ctx(nc, block, stack)
        H = dict(X=X, XB=XB, PS=PS, ones_col_bf=ones_col_bf,
                 ones_row_f32=ones_row_f32, ones_row_bf=ones_row_bf,
                 epsc=epsc, dbg_cw=(dbg_cw if dbg else None),
                 dbg_act=(dbg_act if dbg else None))

        x_v = x_d.rearrange("(s p) t -> p s t", p=128)

        @block.scalar
        def _(sc):
            c.inc("dS", sc.dma_start(out=X[:, 3:6, :], in_=x_v[:, 3:6, :]),
                  16)

        @block.gpsimd
        def _(g):
            c.inc("dW", g.dma_start(out=X[:, 6:8, :], in_=x_v[:, 6:8, :]),
                  16)
        ld_xs = c.n("dS")
        ld_xw = c.n("dW")

        @block.sync
        def _(sy):
            c.inc("dL", sy.dma_start(out=X[:, 0:3, :], in_=x_v[:, 0:3, :]),
                  16)
            for dst, src in ((qk_bias, qk_bias_d), (wo_bias, wo_bias_d),
                             (mask384, mask_d), (id_bf, id_bf_d),
                             (mask_bf, mask_bf_d),
                             (id_f32, id_f32_d), (ones_col_bf, ones_col_bf_d),
                             (ones_row_f32, ones_row_f32_d), (os_t, os_d),
                             (ones_row_bf, ones_row_bf_d),
                             (epsc, epsc_d),
                             (gbias_sb, gbias_d.rearrange(
                                 "p (a b) -> p a b", b=16)),
                             (epsv_sb, epsv_d.rearrange(
                                 "p (a b) -> p a b", b=16)),
                             (epsg_sb, epsg_d.rearrange(
                                 "p (a b) -> p a b", b=4))):
                c.inc("dL", sy.dma_start(out=dst[:], in_=src), 16)

        gw_mark = [0]

        @block.gpsimd
        def _(g):
            gw_mark[0] = c.inc("dW", g.dma_start(
                out=gw_sb[:], in_=gwT_d.rearrange(
                    "(kc p) m -> p kc m", p=128)), 16)
        ld_consts = c.n("dL")

        for dec, eid in ((block.vector, "V"), (block.scalar, "C"),
                         (block.tensor, "T"), (block.gpsimd, "G")):
            @dec
            def _(eng, eid=eid):
                c.w(eng, eid, "dL", ld_consts)
                c.w(eng, eid, "dS", ld_xs)
                c.w(eng, eid, "dW", ld_xw)

        with contextlib.ExitStack() as stA:
            eA = stA.enter_context
            AB1 = eA(nc.sbuf_tensor("AB1", [128, T], F32))
            BB1 = eA(nc.sbuf_tensor("BB1", [128, T], F32))
            sq1 = eA(nc.sbuf_tensor("sq1", [128, NSLAB, T], BF16))
            tmpA1 = eA(nc.sbuf_tensor("tmpA1", [128, T], F32))
            tmpB1 = eA(nc.sbuf_tensor("tmpB1", [128, T], F32))
            srows_a = eA(nc.sbuf_tensor("srows_a", [1, 2, T], F32))
            H.update(AB=AB1, BB=BB1, sq=sq1, tmpA=tmpA1, tmpB=tmpB1,
                     srows=srows_a, epsA=epsc[:, 0:1])
            xb_marks = emit_norm(c, H, "ln")

        with contextlib.ExitStack() as stBC:
            eBC = stBC.enter_context
            qf = eBC(nc.sbuf_tensor("qf", [128, NSLAB, T], BF16))
            kf = eBC(nc.sbuf_tensor("kf", [128, NSLAB, T], BF16))

            with contextlib.ExitStack() as stB1:
                eB = stB1.enter_context
                ropeC = eB(nc.sbuf_tensor("ropeC_s", [128, T], BF16))
                ropeS = eB(nc.sbuf_tensor("ropeS_s", [128, T], BF16))
                pemat = eB(nc.sbuf_tensor("pemat", [128, 4, 128], BF16))
                wtile = [eB(nc.sbuf_tensor(f"wqk{i}", [128, 8, 128], BF16))
                         for i in range(8)]
                rt = [eB(nc.sbuf_tensor(f"rt{i}", [128, 512], F32))
                      for i in range(2)]
                eo_rope = [eB(nc.sbuf_tensor(f"eo{i}", [128, 512], BF16))
                           for i in range(4)]

                @block.sync
                def _(sy):
                    c.dma_guard(sy, "S")
                    c.inc("dL", sy.dma_start(out=ropeC[:], in_=ropeC_d), 16)
                    c.inc("dL", sy.dma_start(out=ropeS[:], in_=ropeS_d), 16)
                    c.inc("dL", sy.dma_start(out=pemat[:], in_=pe_d.rearrange(
                        "m p q -> p m q")), 16)
                ld_rope = c.n("dL")

                def oc_of(st, s):
                    return (0, 4, 8, 12)[st] + s

                wq_marks = {}
                wq_free = {}
                rope_marks = {}
                mm_marks = {}
                rp_marks = {}
                rpe_marks = {}
                prev_eo = {0: None, 1: None}
                prev_bank = {0: None, 1: None}

                @block.gpsimd
                def _(g):
                    if stage < 2:
                        return
                    c.dma_guard(g, "G")
                    for s in range(2):
                        for st in range(4):
                            wq_marks[(s, st)] = c.inc("dW", g.dma_start(
                                out=wtile[(s % 2) * 4 + st][:],
                                in_=wqkv_qk_d[:, sl(oc_of(st, s), 128)]
                                .rearrange("(kc p) m -> p kc m", p=128)), 16)

                keys = []
                for s in range(4 if stage >= 2 else 0):
                    for pair in ("q", "k"):
                        keys.append((s, pair))

                def emit_wtile_load(s):
                    @block.gpsimd
                    def _(g, s=s):
                        for st in range(4):
                            c.w(g, "G", "tS", wq_free[(s - 2, st)])
                            wq_marks[(s, st)] = c.inc("dW", g.dma_start(
                                out=wtile[(s % 2) * 4 + st][:],
                                in_=wqkv_qk_d[:, sl(oc_of(st, s), 128)]
                                .rearrange("(kc p) m -> p kc m", p=128)), 16)

                def emit_qkv_mm(s, pair, tc):
                    key = (s, pair, tc)
                    par = tc % 2
                    st_e, st_o = (0, 1) if pair == "q" else (2, 3)
                    pe_bank, po_bank = PS[par], PS[2 + par]

                    @block.tensor
                    def _(t):
                        _wait_xb(c, t, "T", xb_marks)
                        c.w(t, "T", "dW", wq_marks[(s, st_e)])
                        c.w(t, "T", "dW", wq_marks[(s, st_o)])
                        pk = prev_bank[par]
                        if pk is not None:
                            c.w(t, "T", "vS", rope_marks[pk])
                        for kc in range(NSLAB):
                            ie = t.matmul(
                                pe_bank[:], wtile[(s % 2) * 4 + st_e][:, kc, :],
                                XB[:, kc, sl(tc, 512)],
                                start=(kc == 0), stop=(kc == NSLAB - 1))
                        mm_marks[key + ("e",)] = c.inc("tS", ie)
                        for kc in range(NSLAB):
                            io = t.matmul(
                                po_bank[:], wtile[(s % 2) * 4 + st_o][:, kc, :],
                                XB[:, kc, sl(tc, 512)],
                                start=(kc == 0), stop=(kc == NSLAB - 1))
                        mm_marks[key + ("o",)] = c.inc("tS", io)
                        if tc == NTC - 1:
                            wq_free[(s, st_e)] = c.n("tS")
                            wq_free[(s, st_o)] = c.n("tS")
                        prev_bank[par] = key

                def emit_rope(s, pair, tc):
                    key = (s, pair, tc)
                    par = tc % 2
                    st_e, st_o = (0, 1) if pair == "q" else (2, 3)
                    pe_bank, po_bank = PS[par], PS[2 + par]

                    @block.vector
                    def _(v):
                        c.w(v, "V", "tS", mm_marks[key + ("o",)])
                        c.w(v, "V", "dL", ld_rope)
                        pk = prev_eo[par]
                        if pk is not None:
                            c.w(v, "V", "tS", rp_marks[pk + ("rp",)])
                        be = qk_bias[:, oc_of(st_e, s):oc_of(st_e, s) + 1]
                        bo = qk_bias[:, oc_of(st_o, s):oc_of(st_o, s) + 1]
                        Cs = ropeC[:, sl(tc, 512)]
                        Ss = ropeS[:, sl(tc, 512)]
                        v.scalar_tensor_tensor(rt[0][:], pe_bank[:], be, Cs,
                                               ALU.add, ALU.mult)
                        v.scalar_tensor_tensor(rt[1][:], po_bank[:], bo, Ss,
                                               ALU.add, ALU.mult)
                        v.tensor_tensor(eo_rope[par * 2][:], rt[0][:],
                                        rt[1][:], ALU.subtract)
                        v.scalar_tensor_tensor(rt[0][:], po_bank[:], bo, Cs,
                                               ALU.add, ALU.mult)
                        v.scalar_tensor_tensor(rt[1][:], pe_bank[:], be, Ss,
                                               ALU.add, ALU.mult)
                        rope_marks[key] = c.inc("vS", v.tensor_tensor(
                            eo_rope[par * 2 + 1][:], rt[0][:],
                            rt[1][:], ALU.add))
                        prev_eo[par] = key

                def emit_repack(s, pair, tc):
                    key = (s, pair, tc)
                    par = tc % 2
                    dstf = qf if pair == "q" else kf

                    @block.tensor
                    def _(t):
                        c.w(t, "T", "vS", rope_marks[key])
                        c.w(t, "T", "dL", ld_rope)
                        for u2 in range(2):
                            bank = PS[4 + u2]
                            pk = ("rpe", 4 + u2)
                            if pk in rpe_marks:
                                c.w(t, "T", "vS", rpe_marks[pk])
                            t.matmul(bank[:], pemat[:, u2, :],
                                     eo_rope[par * 2][:],
                                     start=True, stop=False)
                            ii = t.matmul(bank[:], pemat[:, 2 + u2, :],
                                          eo_rope[par * 2 + 1][:],
                                          start=False, stop=True)
                            rp_marks[key + ("rp", u2)] = c.inc("tS", ii)
                        rp_marks[key + ("rp",)] = c.n("tS")

                    @block.vector
                    def _(v):
                        for u2 in range(2):
                            f = 2 * s + u2
                            c.w(v, "V", "tS", rp_marks[key + ("rp", u2)])
                            m = c.inc("vS", v.tensor_copy(
                                dstf[:, f, sl(tc, 512)], PS[4 + u2][:]))
                            rpe_marks[("rpe", 4 + u2)] = m

                flat = [(s, pair, tc) for (s, pair) in keys
                        for tc in range(NTC)]
                for ki, (s, pair, tc) in enumerate(flat):
                    if tc == 0 and pair == "q" and s >= 2:
                        emit_wtile_load(s)
                    emit_qkv_mm(s, pair, tc)
                    emit_rope(s, pair, tc)
                    if ki >= 1:
                        emit_repack(*flat[ki - 1])
                if flat:
                    emit_repack(*flat[-1])

            with contextlib.ExitStack() as stB2:
                eB2 = stB2.enter_context
                wv_sb = eB2(nc.sbuf_tensor("wv_sb", [128, 8, D], BF16))
                vbias_sb = eB2(nc.sbuf_tensor("vbias_sb", [128, D], F32))

                @block.sync
                def _(sy):
                    c.dma_guard(sy, "S")
                    c.inc("dL", sy.dma_start(out=vbias_sb[:], in_=v_bias_d),
                          16)
                ld_vbias = c.n("dL")

                @block.gpsimd
                def _(g):
                    c.dma_guard(g, "G")
                    wq_marks["wv"] = c.inc("dW", g.dma_start(
                        out=wv_sb[:], in_=wqkv_v_d.rearrange(
                            "(kc p) m -> p kc m", p=128)), 16)

                v_mm_marks = {}
                v_ev_marks = {}
                for tb in range(NTB if stage >= 3 else 0):
                    @block.tensor
                    def _(t, tb=tb):
                        c.w(t, "T", "dW", wq_marks["wv"])
                        for half in range(2):
                            gi = tb * 2 + half
                            bank = PS[6 + gi % 2]
                            if gi >= 2:
                                c.w(t, "T", "vS", v_ev_marks[gi - 2])
                            for kc in range(NSLAB):
                                ii = t.matmul(bank[:],
                                              XB[:, kc, sl(tb, 128)],
                                              wv_sb[:, kc, sl(half, 512)],
                                              start=(kc == 0),
                                              stop=(kc == NSLAB - 1))
                            v_mm_marks[gi] = c.inc("tS", ii)

                    @block.vector
                    def _(v, tb=tb):
                        c.w(v, "V", "dL", ld_vbias)
                        c.w(v, "V", "tS", v_mm_marks[tb * 2 + 1])
                        for half in range(2):
                            gi = tb * 2 + half
                            bank = PS[6 + gi % 2]
                            dst = XB[:, 4 * half:4 * half + 4, sl(tb, 128)]
                            v_ev_marks[gi] = c.inc("vS", v.tensor_tensor(
                                dst,
                                bank[:].rearrange("p (a r) -> p a r", r=128),
                                vbias_sb[:, sl(half, 512)].rearrange(
                                    "p (a r) -> p a r", r=128),
                                ALU.add))

            c.barrier()

            with contextlib.ExitStack() as stC:
                eC = stC.enter_context
                wo_sb = eC(nc.sbuf_tensor("wo_sb", [128, 8, D], BF16))
                ao = [eC(nc.sbuf_tensor(f"ao{i}", [128, 8, 128], BF16))
                      for i in range(2)]
                p_bf = [eC(nc.sbuf_tensor(f"pbf{i}", [128, 384], BF16))
                        for i in range(8)]
                pt_sb = [eC(nc.sbuf_tensor(f"ptb{i}", [128, 384], BF16))
                         for i in range(8)]
                mx = [eC(nc.sbuf_tensor(f"mx{i}", [128, 1], F32))
                      for i in range(8)]
                nb = [eC(nc.sbuf_tensor(f"nb{i}", [128, 1], F32))
                      for i in range(8)]
                rs = [eC(nc.sbuf_tensor(f"rs{i}", [128, 1], F32))
                      for i in range(8)]
                rinv = [eC(nc.sbuf_tensor(f"ri{i}", [128, 1], F32))
                        for i in range(8)]

                wo_mark = {}

                @block.gpsimd
                def _(g):
                    c.dma_guard(g, "G")
                    wo_mark["ld"] = c.inc("dW", g.dma_start(
                        out=wo_sb[:],
                        in_=wo_d.rearrange("(kc p) m -> p kc m", p=128)), 16)

                NU = _NQB * HEADS
                um = {}
                SCB = (0, 1, 4, 5)
                ptv = [PS[2][:].bitcast(BF16), PS[3][:].bitcast(BF16)]

                def pt_slot(u):
                    b = ptv[u % 2]
                    off = 512 * ((u // 2) % 2)
                    return b[:, off:off + 384]

                def kbs_of(qb):
                    lo, hi = max(qb - 1, 0), min(qb + 1, NTB - 1)
                    return list(range(lo, hi + 1)), (128 if qb == 0 else 0)

                def vslice(kb, h):
                    s_idx = 4 * (h // 8) + (h % 8) // 2
                    off = kb * 128 + 64 * (h % 2)
                    return XB[:, s_idx, off:off + 64]

                for u in range(NU + 6):
                    @block.tensor
                    def _(t, u=u):
                        if u < NU:
                            qb, h = divmod(u, HEADS)
                            kbs, moff = kbs_of(qb)
                            W = len(kbs) * 128
                            f, u2 = h // 2, h % 2
                            if u >= 4:
                                c.w(t, "T", "cS", um[("exp", u - 4)])
                                c.w(t, "T", "vS", um[("aoev", u - 4)])
                            t.matmul(
                                PS[SCB[u % 4]][:, 0:W],
                                qf[64 * u2:64 * u2 + 64, f, sl(qb, 128)],
                                kf[64 * u2:64 * u2 + 64, f,
                                   kbs[0] * 128:kbs[0] * 128 + W],
                                start=True, stop=False)
                            ii = t.matmul(
                                PS[SCB[u % 4]][:, 0:W], id_bf[:],
                                mask_bf[:, moff:moff + W],
                                start=False, stop=True)
                            um[("sc", u)] = c.inc("tS", ii)
                        if 0 <= u - 2 < NU:
                            up = u - 2
                            qb, h = divmod(up, HEADS)
                            kbs, moff = kbs_of(qb)
                            c.w(t, "T", "vS", um[("norm", up)])
                            if up >= 4:
                                c.w(t, "T", um[("ptev_eng", up - 4)],
                                    um[("ptev", up - 4)])
                            dst = pt_slot(up)
                            for ci in range(len(kbs)):
                                ii = t.transpose(
                                    dst[:, ci * 128:(ci + 1) * 128],
                                    p_bf[up % 8][:, ci * 128:(ci + 1) * 128],
                                    id_bf[:])
                            um[("tr", up)] = c.inc("tS", ii)
                        if 0 <= u - 3 < NU:
                            up = u - 3
                            qb, h = divmod(up, HEADS)
                            kbs, moff = kbs_of(qb)
                            c.w(t, "T", um[("ptev_eng", up)],
                                um[("ptev", up)])
                            if up >= 4:
                                c.w(t, "T", "vS", um[("aoev", up - 4)])
                            for ci, kb in enumerate(kbs):
                                ii = t.matmul(
                                    PS[SCB[up % 4]][0:64, 384:512],
                                    vslice(kb, h),
                                    pt_sb[up % 8][:, sl(ci, 128)],
                                    start=(ci == 0),
                                    stop=(ci == len(kbs) - 1))
                            um[("pv", up)] = c.inc("tS", ii)
                        if u >= 4 and (u - 4) % HEADS == HEADS - 1 and \
                                u - 4 < NU and stage >= 5:
                            qb = (u - 4) // HEADS
                            c.w(t, "T", "vS", um[("aoev", u - 4)])
                            c.w(t, "T", "dW", wo_mark["ld"])
                            if qb >= 1:
                                c.w(t, "T", "vS", um[("xstt_all", qb - 1)])
                            for oc in range(8):
                                bank = PS[6 + oc // 4]
                                reg = bank[:, (oc % 4) * 128:
                                           (oc % 4) * 128 + 128]
                                for kc in range(NSLAB):
                                    ii = t.matmul(
                                        reg, wo_sb[:, kc, sl(oc, 128)],
                                        ao[qb % 2][:, kc, :],
                                        start=(kc == 0),
                                        stop=(kc == NSLAB - 1))
                                um[("wo", qb, oc)] = c.inc("tS", ii)
                            um[("wo_all", qb)] = c.n("tS")

                    @block.vector
                    def _(v, u=u):
                        if u < NU:
                            qb, h = divmod(u, HEADS)
                            kbs, moff = kbs_of(qb)
                            W = len(kbs) * 128
                            c.w(v, "V", "tS", um[("sc", u)])
                            rmk = c.inc("vS", v.tensor_reduce(
                                mx[u % 8][:], PS[SCB[u % 4]][:, 0:W],
                                AX.X, ALU.max, negate=True))
                            c.w(v, "V", "vS", rmk)
                            um[("mask", u)] = c.inc("vS", v.tensor_scalar(
                                nb[u % 8][:], mx[u % 8][:], 0.125, None,
                                ALU.mult))

                    @block.scalar
                    def _(sc, u=u):
                        if u < NU:
                            qb, h = divmod(u, HEADS)
                            kbs, moff = kbs_of(qb)
                            W = len(kbs) * 128
                            c.w(sc, "C", "vS", um[("mask", u)])
                            if u >= 8:
                                c.w(sc, "C", "tS", um[("tr", u - 8)])
                                c.w(sc, "C", "vS", um[("recip", u - 8)])
                            um[("exp", u)] = c.inc("cS", sc.activation(
                                p_bf[u % 8][:, 0:W], PS[SCB[u % 4]][:, 0:W],
                                AF.Exp, bias=nb[u % 8][:], scale=0.125,
                                accum_out=rs[u % 8][:]))

                    @block.vector
                    def _(v, u=u):
                        if 0 <= u - 3 < NU:
                            up = u - 3
                            qb, h = divmod(up, HEADS)
                            c.w(v, "V", "tS", um[("pv", up)])
                            if qb >= 2 and stage >= 5:
                                c.w(v, "V", "tS", um[("wo_all", qb - 2)])
                            um[("aoev", up)] = c.inc("vS", v.tensor_copy(
                                ao[qb % 2][64 * (h % 2):64 * (h % 2) + 64,
                                           h // 2, :],
                                PS[SCB[up % 4]][0:64, 384:512]))
                        if 0 <= u - 2 < NU:
                            up = u - 2
                            qb, h = divmod(up, HEADS)
                            kbs, _m = kbs_of(qb)
                            W = len(kbs) * 128
                            c.w(v, "V", "tS", um[("tr", up)])
                            if up >= 8:
                                c.w(v, "V", "tS", um[("pv", up - 8)])
                            um[("ptev", up)] = c.inc("vS", v.tensor_copy(
                                pt_sb[up % 8][:, 0:W], pt_slot(up)[:, 0:W]))
                            um[("ptev_eng", up)] = "vS"
                        if 0 <= u - 1 < NU:
                            up = u - 1
                            qb, h = divmod(up, HEADS)
                            kbs, _m = kbs_of(qb)
                            W = len(kbs) * 128
                            c.w(v, "V", "cS", um[("exp", up)])
                            um[("recip", up)] = c.inc("vS", v.reciprocal(
                                rinv[up % 8][:], rs[up % 8][:]))
                            c.w(v, "V", "vS", um[("recip", up)])
                            um[("norm", up)] = c.inc("vS", v.tensor_scalar(
                                p_bf[up % 8][:, 0:W], p_bf[up % 8][:, 0:W],
                                rinv[up % 8][:], None, ALU.mult))
                        if u >= 5 and (u - 5) % HEADS == HEADS - 1 and \
                                u - 5 < NU and stage >= 5:
                            qb = (u - 5) // HEADS
                            for oc in range(8):
                                bank = PS[6 + oc // 4]
                                reg = bank[:, (oc % 4) * 128:
                                           (oc % 4) * 128 + 128]
                                c.w(v, "V", "tS", um[("wo_all", qb)])
                                ii = v.scalar_tensor_tensor(
                                    X[:, oc, sl(qb, 128)], reg,
                                    wo_bias[:, oc:oc + 1],
                                    X[:, oc, sl(qb, 128)], ALU.add, ALU.add)
                            um[("xstt_all", qb)] = c.inc("vS", ii)

            if dbg:
                @block.sync
                def _(sy):
                    for nm, val in (("tS", c.n("tS")), ("vS", c.n("vS")),
                                    ("cS", c.n("cS")), ("gS", c.n("gS"))):
                        c.w(sy, "S", nm, val)
                    c.inc("dO", sy.dma_start(out=dbg_xb, in_=XB[:]), 16)
                    c.inc("dO", sy.dma_start(out=dbg_qf, in_=qf[:]), 16)
                    c.inc("dO", sy.dma_start(out=dbg_kf, in_=kf[:]), 16)
                    if stage >= 4:
                        c.inc("dO", sy.dma_start(
                            out=dbg_ao, in_=ao[0][:]), 16)
                        c.inc("dO", sy.dma_start(
                            out=dbg_pt, in_=pt_sb[0][:]), 16)
                        c.inc("dO", sy.dma_start(
                            out=dbg_pb, in_=p_bf[0][:]), 16)
                        c.inc("dO", sy.dma_start(
                            out=dbg_sc, in_=sc_sb[0][:]), 16)
                        for di_, src_ in enumerate((mx, nb, rs, rinv)):
                            c.inc("dO", sy.dma_start(
                                out=dbg_s4[:, di_:di_ + 1], in_=src_[0][:],
                                allow_slow_non_contiguous=True), 16)
                    sy.wait_ge(c.sem["dO"][0], c.n("dO"))

            c.barrier()

        if upto >= 2:
            emit_ffn(c, H, 0, dict(gpT=gpT_d, upT=upT_d, dnT=dnT_d),
                     cw_dram, gw_sb, gbias_sb, epsv_sb, epsg_sb, id_f32,
                     gw_mark[0])
            c.barrier()
        if upto >= 3:
            emit_attnb(c, H, dict(wmT=wmT_d, wpT=wpT_d, v0=v0_d, v1=v1_d),
                       os_t)
            c.barrier()
        if upto >= 4:
            emit_ffn(c, H, 1, dict(gpT=gpT_d, upT=upT_d, dnT=dnT_d),
                     cw_dram, gw_sb, gbias_sb, epsv_sb, epsg_sb, id_f32,
                     gw_mark[0])
            c.barrier()

        out_v = out_d.rearrange("(s p) t -> p s t", p=128)

        @block.scalar
        def _(sc):
            c.inc("dS", sc.dma_start(out=out_v[:, 3:6, :],
                                     in_=X[:, 3:6, :]), 16)

        @block.gpsimd
        def _(g):
            c.inc("dW", g.dma_start(out=out_v[:, 6:8, :],
                                    in_=X[:, 6:8, :]), 16)

        @block.sync
        def _(sy):
            c.inc("dO", sy.dma_start(
                out=out_v[:, 0:3, :], in_=X[:, 0:3, :]), 16)
            sy.wait_ge(c.sem["dS"][0], c.n("dS"))
            sy.wait_ge(c.sem["dW"][0], c.n("dW"))
            sy.wait_ge(c.sem["dO"][0], c.n("dO"))

    return nc


def emit_ffn(c, H, fidx, wd_dram, cw_dram, gw_sb, gbias_sb, epsv_sb, epsg_sb,
             id_f32, gw_mark):
    nc, bl = c.nc, c.block
    X, XB, PS = H["X"], H["XB"], H["PS"]

    with contextlib.ExitStack() as stD:
        eD = stD.enter_context

        wg = eD(nc.sbuf_tensor(f"wg{fidx}", [128, 8, FI], BF16))
        em = {}
        e0 = 0

        @bl.gpsimd
        def _(g):
            c.dma_guard(g, "G")
            em[("wgld", e0)] = c.inc("dW", g.dma_start(
                out=wg[:], in_=wd_dram["gpT"][e0:e0 + 1].rearrange(
                    "o (kc p) i -> p (o kc) i", p=128)), 16)

        with contextlib.ExitStack() as stLN:
            eL = stLN.enter_context
            AB = eL(nc.sbuf_tensor(f"ABd{fidx}", [128, T], F32))
            BB = eL(nc.sbuf_tensor(f"BBd{fidx}", [128, T], F32))
            sq = eL(nc.sbuf_tensor(f"sqd{fidx}", [128, NSLAB, T], BF16))
            tmpA = eL(nc.sbuf_tensor(f"tmpAd{fidx}", [128, T], F32))
            tmpB = eL(nc.sbuf_tensor(f"tmpBd{fidx}", [128, T], F32))
            srows_d = eL(nc.sbuf_tensor(f"srows_d{fidx}", [1, 2, T], F32))
            H2 = dict(H)
            H2.update(AB=AB, BB=BB, sq=sq, tmpA=tmpA, tmpB=tmpB,
                      srows=srows_d, epsA=H["epsc"][:, 0:1])
            xb_marks = emit_norm(c, H2, "ln")

        act = eD(nc.sbuf_tensor(f"act{fidx}", [128, FIC, 1024], BF16))
        wu = eD(nc.sbuf_tensor(f"wu{fidx}", [128, 8, FI], BF16))
        WBt = eD(nc.sbuf_tensor(f"WB{fidx}", [128, T], BF16))
        wrows = eD(nc.sbuf_tensor(f"wr{fidx}", [128, T], BF16))
        silu = [eD(nc.sbuf_tensor(f"sil{fidx}_{i}", [128, 512], BF16))
                for i in range(4)]
        uw = [eD(nc.sbuf_tensor(f"uw{fidx}_{i}", [128, 512], BF16))
              for i in range(4)]
        dtmp_box = []
        wd_box = []
        wrow = [wrows[0:1, :], wrows[64:65, :]]

        @bl.gpsimd
        def _(g):
            c.dma_guard(g, "G")
            em[("wuld", e0)] = c.inc("dW", g.dma_start(
                out=wu[:], in_=wd_dram["upT"][e0:e0 + 1].rearrange(
                    "o (kc p) i -> p (o kc) i", p=128)), 16)

        gm = {}
        stG = contextlib.ExitStack()
        eG = stG.enter_context
        lg = eG(nc.sbuf_tensor(f"lg{fidx}", [128, 16, 16], F32))
        cw = eG(nc.sbuf_tensor(f"cw{fidx}", [128, 16, 16], F32))
        gsc = eG(nc.sbuf_tensor(f"gsc{fidx}", [128, 16, 16], F32))
        sfc = eG(nc.sbuf_tensor(f"sfc{fidx}", [128, 16, 16], F32))
        gtmp = eG(nc.sbuf_tensor(f"gtmp{fidx}", [128, 16, 16], F32))
        gs = eG(nc.sbuf_tensor(f"gs{fidx}", [128, 16, 4], F32))
        gt2 = eG(nc.sbuf_tensor(f"gt2{fidx}", [128, 16, 4], F32))
        selg = eG(nc.sbuf_tensor(f"selg{fidx}", [128, 16, 4], F32))
        sele = eG(nc.sbuf_tensor(f"sele{fidx}", [128, 16, 16], F32))
        red = eG(nc.sbuf_tensor(f"red{fidx}", [128, 16], F32))
        red2 = eG(nc.sbuf_tensor(f"red2{fidx}", [128, 16], F32))
        cwT = eG(nc.sbuf_tensor(f"cwT{fidx}", [16, 16, 128], F32))

        @bl.tensor
        def _(t):
            _wait_xb(c, t, "T", xb_marks)
            c.w(t, "T", "dW", gw_mark)
            for tt in range(NTB):
                for kc in range(NSLAB):
                    ii = t.matmul(PS[4][:, sl(tt, 16)],
                                  XB[:, kc, sl(tt, 128)],
                                  gw_sb[:, kc, :],
                                  start=(kc == 0), stop=(kc == NSLAB - 1))
            gm["lg"] = c.inc("tS", ii)

        @bl.vector
        def _(v):
            c.w(v, "V", "tS", gm["lg"])
            gm["lgev"] = c.inc("vS", v.tensor_copy(
                lg[:].rearrange("p a b -> p (a b)"), PS[4][:, 0:256]))

        @bl.scalar
        def _(sc):
            c.w(sc, "C", "vS", gm["lgev"])
            gm["sig"] = c.inc("cS", sc.activation(gsc[:], lg[:], AF.Sigmoid))

        @bl.vector
        def _(v):
            c.w(v, "V", "cS", gm["sig"])
            v.tensor_tensor(sfc[:], gsc[:], gbias_sb[:], ALU.add)
            g4 = sfc[:].rearrange("p a (g e) -> p a g e", e=4)
            pairs = [(0, 1), (0, 2), (0, 3), (1, 2), (1, 3), (2, 3)]
            i0, j0 = pairs[0]
            v.tensor_tensor(gs[:], g4[:, :, :, i0], g4[:, :, :, j0], ALU.add)
            for (i, j) in pairs[1:]:
                v.tensor_tensor(gt2[:], g4[:, :, :, i], g4[:, :, :, j],
                                ALU.add)
                v.tensor_tensor(gs[:], gs[:], gt2[:], ALU.max)
            v.tensor_tensor(gs[:], gs[:], epsg_sb[:], ALU.add)
            v.memset(selg[:], 0.0)
            for _it in range(2):
                c.w(v, "V", "vS", c.inc("vS", v.tensor_reduce(
                    red[:], gs[:], AX.X, ALU.max)))
                mxb = red[:].unsqueeze(2).to_broadcast([128, 16, 4])
                v.tensor_tensor(gt2[:], gs[:], mxb, ALU.is_equal)
                v.tensor_tensor(selg[:], selg[:], gt2[:], ALU.add)
                v.scalar_tensor_tensor(gs[:], gt2[:], NEG, gs[:],
                                       ALU.mult, ALU.add)
            sm = selg[:].unsqueeze(3).to_broadcast([128, 16, 4, 4])
            v.tensor_tensor(gtmp[:].rearrange("p a (g e) -> p a g e", e=4),
                            sfc[:].rearrange("p a (g e) -> p a g e", e=4),
                            sm, ALU.mult)
            v.tensor_tensor(gtmp[:], gtmp[:], epsv_sb[:], ALU.add)
            v.memset(sele[:], 0.0)
            for _it in range(4):
                c.w(v, "V", "vS", c.inc("vS", v.tensor_reduce(
                    red[:], gtmp[:], AX.X, ALU.max)))
                mxb = red[:].unsqueeze(2).to_broadcast([128, 16, 16])
                v.tensor_tensor(sfc[:], gtmp[:], mxb, ALU.is_equal)
                v.tensor_tensor(sele[:], sele[:], sfc[:], ALU.add)
                v.scalar_tensor_tensor(gtmp[:], sfc[:], NEG, gtmp[:],
                                       ALU.mult, ALU.add)
            v.tensor_tensor(cw[:], gsc[:], sele[:], ALU.mult)
            c.w(v, "V", "vS", c.inc("vS", v.tensor_reduce(
                red[:], cw[:], AX.X, ALU.add)))
            v.tensor_scalar(red[:], red[:], 1e-20, None, ALU.add)
            c.w(v, "V", "vS", c.inc("vS", v.reciprocal(red2[:], red[:])))
            gm["cw"] = c.inc("vS", v.tensor_tensor(
                cw[:], cw[:],
                red2[:].unsqueeze(2).to_broadcast([128, 16, 16]),
                ALU.mult))

        trm = {}

        def emit_cw_store():
            for tt in range(NTB):
                bank = PS[5 + tt % 2][:]

                @bl.tensor
                def _(t, tt=tt, bank=bank):
                    c.w(t, "T", "vS", gm["cw"])
                    if tt >= 2:
                        c.w(t, "T", "vS", trm[("ev", tt - 2)])
                    trm[("tr", tt)] = c.inc("tS", t.transpose(
                        bank[0:16, 0:128], cw[:, tt, :], id_f32[:]))

                @bl.vector
                def _(v, tt=tt, bank=bank):
                    c.w(v, "V", "tS", trm[("tr", tt)])
                    trm[("ev", tt)] = c.inc("vS", v.tensor_copy(
                        cwT[:, tt, :], bank[0:16, 0:128]))

            @bl.sync
            def _(sy):
                c.w(sy, "S", "vS", trm[("ev", NTB - 1)])
                gm["cwst"] = c.inc("dC", sy.dma_start(
                    out=cw_dram, in_=cwT[:].rearrange("p a b -> p (a b)")),
                    16)
                if H.get("dbg_cw") is not None and fidx == 0:
                    c.inc("dC", sy.dma_start(
                        out=H["dbg_cw"], in_=cwT[:].rearrange(
                            "p a b -> p (a b)")), 16)
            stG.close()

        with contextlib.ExitStack() as stE:
            def dma_w(g, ei, prev=None):
                if prev is not None:
                    c.w(g, "G", "tS", em[("gmm_all", prev)])
                em[("wgld", ei)] = c.inc("dW", g.dma_start(
                    out=wg[:], in_=wd_dram["gpT"][ei:ei + 1].rearrange(
                        "o (kc p) i -> p (o kc) i", p=128)), 16)
                if prev is not None:
                    c.w(g, "G", "tS", em[("umm_all", prev)])
                em[("wuld", ei)] = c.inc("dW", g.dma_start(
                    out=wu[:], in_=wd_dram["upT"][ei:ei + 1].rearrange(
                        "o (kc p) i -> p (o kc) i", p=128)), 16)
                if prev is not None:
                    c.w(g, "G", "tS", em[("dmm_all", prev)])
                em[("wdld", ei)] = c.inc("dW", g.dma_start(
                    out=wd_box[0][:],
                    in_=wd_dram["dnT"][ei:ei + 1].rearrange(
                        "o (kc p) h -> p (o kc) h", p=128)), 16)

            esel = list(range(NE)) if _ESEL is None else list(_ESEL)

            def dma_w_sel(g, pos):
                if pos < len(esel):
                    dma_w(g, esel[pos])

            def emit_wrow(ei):
                par = ei % 2

                @bl.gpsimd
                def _(g, ei=ei, par=par):
                    c.dma_guard(g, "G")
                    c.w(g, "G", "dC", gm["cwst"])
                    if ("wbev", ei - 2) in em:
                        c.w(g, "G", "vS", em[("wbev", ei - 2)])
                    em[("wrow", ei)] = c.inc("dW", g.dma_start(
                        out=wrow[par], in_=cw_dram[ei:ei + 1, :]), 16)

            def emit_wb(ei):
                par = ei % 2
                for ch in range(NTC):
                    @bl.tensor
                    def _(t, ei=ei, ch=ch, par=par):
                        c.w(t, "T", "dW", em[("wrow", ei)])
                        bank = PS[6 + ch % 2]
                        pk = ("wbbank", ch % 2)
                        if pk in em:
                            c.w(t, "T", "vS", em[pk])
                        elif ("ev", NTB - 1) in trm:
                            c.w(t, "T", "vS", trm[("ev", NTB - 1)])
                        em[("wbmm", ei, ch)] = c.inc("tS", t.matmul(
                            bank[:],
                            H["ones_row_bf"][0:1, :] if par == 0 else
                            H["ones_row_bf"][64:65, :],
                            wrow[par][:, sl(ch, 512)],
                            start=True, stop=True))

                    @bl.vector
                    def _(v, ei=ei, ch=ch, par=par):
                        c.w(v, "V", "tS", em[("wbmm", ei, ch)])
                        bank = PS[6 + ch % 2]
                        em[("wbbank", ch % 2)] = c.inc(
                            "vS", v.tensor_copy(WBt[:, sl(ch, 512)],
                                                bank[:]))
                        if ch == NTC - 1:
                            em[("wbev", ei)] = em[("wbbank", ch % 2)]

            for epos, ei in enumerate(esel):
                routed = ei < 16
                nxt = esel[epos + 1] if epos + 1 < len(esel) else None
                for th in range(2):
                    if th == 0 and epos > 0 and nxt is not None and nxt < 16:
                        emit_wrow(nxt)
                    raw0 = (epos == 0 and th == 0 and routed)
                    for i in range(FIC):
                        for tc2 in range(2):
                            tcg = th * 2 + tc2
                            gi = i * 2 + tc2
                            gbank = PS[gi % 2]
                            ubank = PS[2 + gi % 2]

                            @bl.tensor
                            def _(t, ei=ei, th=th, i=i, tc2=tc2, tcg=tcg,
                                  gi=gi, gbank=gbank, ubank=ubank):
                                c.w(t, "T", "dW", em[("wgld", ei)])
                                _wait_xb(c, t, "T", xb_marks)
                                pk = ("gprev", gi % 2)
                                if pk in em:
                                    c.w(t, "T", em[pk][0], em[pk][1])
                                for kc in range(NSLAB):
                                    ii = t.matmul(
                                        gbank[:], wg[:, kc, sl(i, 128)],
                                        XB[:, kc, sl(tcg, 512)],
                                        start=(kc == 0),
                                        stop=(kc == NSLAB - 1))
                                em[("gmm", ei, th, i, tc2)] = c.inc("tS", ii)
                                c.w(t, "T", "dW", em[("wuld", ei)])
                                pk = ("uprev", gi % 2)
                                if pk in em:
                                    c.w(t, "T", em[pk][0], em[pk][1])
                                for kc in range(NSLAB):
                                    ii = t.matmul(
                                        ubank[:], wu[:, kc, sl(i, 128)],
                                        XB[:, kc, sl(tcg, 512)],
                                        start=(kc == 0),
                                        stop=(kc == NSLAB - 1))
                                em[("umm", ei, th, i, tc2)] = c.inc("tS", ii)

                            @bl.scalar
                            def _(sc, ei=ei, th=th, i=i, tc2=tc2, gi=gi,
                                  gbank=gbank):
                                c.w(sc, "C", "tS", em[("gmm", ei, th, i,
                                                       tc2)])
                                j = gi % 4
                                pk = ("silu_used", j)
                                if pk in em:
                                    c.w(sc, "C", "gS", em[pk])
                                em[("silu", ei, th, i, tc2)] = c.inc(
                                    "cS", sc.activation(
                                        silu[j][:], gbank[:], AF.Silu))
                                em[("gprev", gi % 2)] = \
                                    ("cS", em[("silu", ei, th, i, tc2)])

                            if raw0:
                                @bl.scalar
                                def _(sc, ei=ei, th=th, i=i, tc2=tc2, gi=gi,
                                      ubank=ubank):
                                    c.w(sc, "C", "tS", em[("umm", ei, th, i,
                                                           tc2)])
                                    j = gi % 4
                                    pk = ("uw_used", j)
                                    if pk in em:
                                        c.w(sc, "C", "gS", em[pk])
                                    ii = sc.copy(uw[j][:], ubank[:])
                                    em[("uwv", ei, th, i, tc2)] = \
                                        c.inc("cS", ii)
                                    em[("uwv_eng", ei, th, i, tc2)] = "cS"
                                    em[("uprev", gi % 2)] = \
                                        ("cS", em[("uwv", ei, th, i, tc2)])
                            else:
                                @bl.vector
                                def _(v, ei=ei, th=th, i=i, tc2=tc2, tcg=tcg,
                                      gi=gi, ubank=ubank, routed=routed):
                                    c.w(v, "V", "tS", em[("umm", ei, th, i,
                                                          tc2)])
                                    j = gi % 4
                                    pk = ("uw_used", j)
                                    if pk in em:
                                        c.w(v, "V", "gS", em[pk])
                                    if routed:
                                        c.w(v, "V", "vS", em[("wbev", ei)])
                                        ii = v.tensor_tensor(
                                            uw[j][:], ubank[:],
                                            WBt[:, sl(tcg, 512)], ALU.mult)
                                    else:
                                        ii = v.tensor_copy(uw[j][:],
                                                           ubank[:])
                                    em[("uwv", ei, th, i, tc2)] = \
                                        c.inc("vS", ii)
                                    em[("uwv_eng", ei, th, i, tc2)] = "vS"
                                    em[("uprev", gi % 2)] = \
                                        ("vS", em[("uwv", ei, th, i, tc2)])

                            @bl.gpsimd
                            def _(g, ei=ei, th=th, i=i, tc2=tc2, gi=gi):
                                j = gi % 4
                                c.w(g, "G", "cS", em[("silu", ei, th, i,
                                                      tc2)])
                                c.w(g, "G", em[("uwv_eng", ei, th, i, tc2)],
                                    em[("uwv", ei, th, i,
                                        tc2)])
                                pk = ("actused",)
                                if pk in em:
                                    c.w(g, "G", "tS", em[pk])
                                ii = g.tensor_tensor(
                                    act[:, i, sl(tc2, 512)], silu[j][:],
                                    uw[j][:], ALU.mult)
                                mk = c.inc("gS", ii)
                                em[("act", ei, th, i, tc2)] = mk
                                em[("silu_used", j)] = mk
                                em[("uw_used", j)] = mk

                    if raw0:
                        emit_cw_store()
                        wd_box.append(eD(nc.sbuf_tensor(
                            f"wd{fidx}", [128, FIC, D], BF16)))
                        dtmp_box.append(eD(nc.sbuf_tensor(
                            f"dtmp{fidx}", [128, 512], BF16)))

                        @bl.gpsimd
                        def _(g, ei=ei):
                            c.dma_guard(g, "G")
                            em[("wdld", ei)] = c.inc("dW", g.dma_start(
                                out=wd_box[0][:],
                                in_=wd_dram["dnT"][ei:ei + 1].rearrange(
                                    "o (kc p) h -> p (o kc) h", p=128)), 16)
                        emit_wrow(ei)
                        emit_wb(ei)
                        if nxt is not None and nxt < 16:
                            emit_wrow(nxt)
                    if th == 1 and nxt is not None and nxt < 16:
                        emit_wb(nxt)

                    for h in range(8):
                        for tc2 in range(2):
                            tcg = th * 2 + tc2
                            gi = h * 2 + tc2
                            bank = PS[4 + gi % 2]

                            @bl.tensor
                            def _(t, ei=ei, th=th, h=h, tc2=tc2, tcg=tcg,
                                  gi=gi, bank=bank, raw0=raw0):
                                c.w(t, "T", "dW", em[("wdld", ei)])
                                if raw0 and h == 0 and tc2 == 0:
                                    c.w(t, "T", "vS", trm[("ev", NTB - 1)])
                                pk = ("dprev", gi % 2)
                                if pk in em:
                                    c.w(t, "T", "vS", em[pk])
                                for i in range(FIC):
                                    c.w(t, "T", "gS",
                                        em[("act", ei, th, i, tc2)])
                                    ii = t.matmul(
                                        bank[:],
                                        wd_box[0][:, i, sl(h, 128)],
                                        act[:, i, sl(tc2, 512)],
                                        start=(i == 0), stop=(i == FIC - 1))
                                em[("dmm", ei, th, h, tc2)] = c.inc("tS", ii)
                                if h == 7 and tc2 == 1:
                                    em[("actused",)] = c.n("tS")

                            @bl.vector
                            def _(v, ei=ei, th=th, h=h, tc2=tc2, tcg=tcg,
                                  gi=gi, bank=bank, raw0=raw0):
                                c.w(v, "V", "tS", em[("dmm", ei, th, h, tc2)])
                                if raw0:
                                    dtmp = dtmp_box[0]
                                    c.w(v, "V", "vS", em[("wbev", ei)])
                                    v.tensor_tensor(
                                        dtmp[:], bank[:],
                                        WBt[:, sl(tcg, 512)], ALU.mult)
                                    ii = v.tensor_tensor(
                                        X[:, h, sl(tcg, 512)],
                                        X[:, h, sl(tcg, 512)], dtmp[:],
                                        ALU.add)
                                else:
                                    ii = v.tensor_tensor(
                                        X[:, h, sl(tcg, 512)],
                                        X[:, h, sl(tcg, 512)], bank[:],
                                        ALU.add)
                                em[("dprev", gi % 2)] = c.inc("vS", ii)

                    if th == 1 and H.get("dbg_act") is not None and \
                            fidx == 0 and epos == len(esel) - 1:
                        @bl.sync
                        def _(sy, ei=ei):
                            for i2 in range(FIC):
                                for t2 in range(2):
                                    c.w(sy, "S", "gS",
                                        em[("act", ei, 1, i2, t2)])
                            c.inc("dO", sy.dma_start(out=H["dbg_act"],
                                                     in_=act[:]), 16)
                            sy.wait_ge(c.sem["dO"][0], c.n("dO"))
                    if th == 1:
                        em[("gmm_all", ei)] = em[("gmm", ei, 1, FIC - 1, 1)]
                        em[("umm_all", ei)] = em[("umm", ei, 1, FIC - 1, 1)]
                        em[("dmm_all", ei)] = em[("dmm", ei, 1, 7, 1)]

                        @bl.gpsimd
                        def _(g, ei=ei, epos=epos):
                            if epos + 1 < len(esel):
                                dma_w(g, esel[epos + 1], prev=ei)



def emit_attnb(c, H, wdr, os_t):
    nc, bl = c.nc, c.block
    X, XB, PS = H["X"], H["XB"], H["PS"]
    NQ = 4
    NCH = 32
    L = 16

    with contextlib.ExitStack() as stE:
        eE = stE.enter_context
        AB = eE(nc.sbuf_tensor("ABe", [128, T], F32))
        BB = eE(nc.sbuf_tensor("BBe", [128, T], F32))
        wm_sb = eE(nc.sbuf_tensor("wm_sb", [128, 8, D], BF16))
        wp_sb = eE(nc.sbuf_tensor("wp_sb", [128, 4, D], BF16))
        wa = eE(nc.sbuf_tensor("wa", [128, 2, NCH], F32))
        wb_ = eE(nc.sbuf_tensor("wb_", [128, 2, NCH], F32))
        carry = eE(nc.sbuf_tensor("carry", [128, 2, 2], F32))
        v0_sb = eE(nc.sbuf_tensor("v0_sb", [128, 2], F32))
        v1_sb = eE(nc.sbuf_tensor("v1_sb", [128, 2], F32))
        em = {}

        @bl.sync
        def _(sy):
            c.dma_guard(sy, "S")
            c.inc("dL", sy.dma_start(out=v0_sb[:], in_=wdr["v0"]), 16)
            c.inc("dL", sy.dma_start(out=v1_sb[:], in_=wdr["v1"]), 16)
        ld_v = c.n("dL")

        @bl.gpsimd
        def _(g):
            c.dma_guard(g, "G")
            em["wmld"] = c.inc("dW", g.dma_start(
                out=wm_sb[:], in_=wdr["wmT"].rearrange(
                    "(kc p) m -> p kc m", p=128)), 16)
            em["wpld"] = c.inc("dW", g.dma_start(
                out=wp_sb[:], in_=wdr["wpT"].rearrange(
                    "(kc p) m -> p kc m", p=128)), 16)

        with contextlib.ExitStack() as stR:
            eR = stR.enter_context
            sq = eR(nc.sbuf_tensor("sqe", [128, NSLAB, T], BF16))
            srows_e = eR(nc.sbuf_tensor("srows_e", [1, 2, T], F32))
            tmpAe = eR(nc.sbuf_tensor("tmpAe", [128, T], F32))
            tmpBe = eR(nc.sbuf_tensor("tmpBe", [128, T], F32))
            H2 = dict(H)
            H2.update(AB=AB, BB=BB, sq=sq, tmpA=tmpAe, tmpB=tmpBe,
                      srows=srows_e, epsA=H["epsc"][:, 0:1])
            xn_marks = emit_norm(c, H2, "ln")

        @bl.vector
        def _(v):
            if True:
                c.w(v, "V", "vS", max(xn_marks[:5]))
                c.w(v, "V", "gS", max(xn_marks[5:]))
                c.inc("vS", v.tensor_scalar(AB[:], AB[:], 1.0, None,
                                            ALU.add))
        ab1_mark = c.n("vS")

        M1 = eE(nc.sbuf_tensor("M1", [128, NSLAB, 512], F32))
        M2 = eE(nc.sbuf_tensor("M2", [128, NSLAB, 512], F32))
        n2 = eE(nc.sbuf_tensor("n2", [128, 2, 512], F32))
        n2b = eE(nc.sbuf_tensor("n2b", [128, 2, 512], F32))
        hb0 = eE(nc.sbuf_tensor("hb0", [128, 2, 512], BF16))
        hb1 = eE(nc.sbuf_tensor("hb1", [128, 2, 512], BF16))
        tq0 = eE(nc.sbuf_tensor("tq0", [128, 2, 512], F32))
        tq1 = eE(nc.sbuf_tensor("tq1", [128, 2, 512], F32))

        def plane(buf, p, khslice=slice(None), tslice=slice(None)):
            v4 = buf[:].rearrange("p (a k) t -> p a k t", k=2)
            return v4[:, p, khslice, tslice]

        for q4 in range(NQ):
            for oc in range(8):
                bank = PS[oc % 2]

                @bl.tensor
                def _(t, q4=q4, oc=oc, bank=bank):
                    c.w(t, "T", "dW", em["wmld"])
                    _wait_xb(c, t, "T", xn_marks)
                    if oc >= 2 or q4 > 0:
                        c.w(t, "T", "vS", em.get(("mev", q4, oc - 2),
                                                 em.get(("scan_done", q4 - 1),
                                                        0)))
                    for kc in range(NSLAB):
                        ii = t.matmul(bank[:], wm_sb[:, kc, sl(oc, 128)],
                                      XB[:, kc, sl(q4, 512)],
                                      start=(kc == 0), stop=(kc == NSLAB - 1))
                    em[("mmm", q4, oc)] = c.inc("tS", ii)

                @bl.vector
                def _(v, q4=q4, oc=oc, bank=bank):
                    c.w(v, "V", "tS", em[("mmm", q4, oc)])
                    if q4 > 0:
                        c.w(v, "V", "vS", em.get(("scan_done", q4 - 1), 0))
                    em[("mev", q4, oc)] = c.inc("vS", v.tensor_copy(
                        M1[:, oc, :], bank[:]))

            @bl.vector
            def _(v, q4=q4):
                c.w(v, "V", "vS", em[("mev", q4, 7)])
                v.tensor_tensor(n2[:], plane(M1, 0), plane(M1, 0), ALU.mult)
                for p in range(1, 4):
                    v.tensor_tensor(n2b[:], plane(M1, p), plane(M1, p),
                                    ALU.mult)
                    ii = v.tensor_tensor(n2[:], n2[:], n2b[:], ALU.add)
                em[("n2", q4)] = c.inc("vS", ii)

            @bl.scalar
            def _(sc, q4=q4):
                c.w(sc, "C", "vS", em[("n2", q4)])
                lnm = c.inc("cS", sc.activation(n2[:], n2[:], AF.Ln))
                c.w(sc, "C", "cS", lnm)
                em[("inv", q4)] = c.inc("cS", sc.activation(
                    n2[:], n2[:], AF.Exp, scale=-0.5))

            @bl.vector
            def _(v, q4=q4):
                c.w(v, "V", "cS", em[("inv", q4)])
                v.tensor_scalar(n2[:], n2[:], 1e6, None, ALU.min)
                for p in range(4):
                    ii = v.tensor_tensor(plane(M1, p), plane(M1, p), n2[:],
                                         ALU.mult)
                em[("mnorm", q4)] = c.inc("vS", ii)

            def c4(buf, p, l):
                return plane(buf, p).rearrange(
                    "p k (c l) -> p k c l", l=L)[:, :, :, l:l + 1]

            ts0 = tq0[:].rearrange("p k (c l) -> p k c l", l=L)[:, :, :, 0:1]
            ts1 = tq1[:].rearrange("p k (c l) -> p k c l", l=L)[:, :, :, 0:1]

            @bl.vector
            def _(v, q4=q4):
                c.w(v, "V", "vS", em[("mnorm", q4)])
                for p in range(4):
                    v.tensor_copy(c4(M2, p, 0), c4(M1, p, 0))
                for l in range(1, L):
                    for (o, a, pp, b2, qq) in ((0, 0, 0, 1, 2),
                                               (1, 0, 1, 1, 3),
                                               (2, 2, 0, 3, 2),
                                               (3, 2, 1, 3, 3)):
                        v.tensor_tensor(ts0, c4(M1, a, l), c4(M2, pp, l - 1),
                                        ALU.mult)
                        v.tensor_tensor(ts1, c4(M1, b2, l), c4(M2, qq, l - 1),
                                        ALU.mult)
                        v.tensor_tensor(c4(M2, o, l), ts0, ts1, ALU.add)
                em[("scan", q4)] = c.inc("vS", v.tensor_copy(
                    wa[:, :, 0:1],
                    (v0_sb[:].unsqueeze(2) if q4 == 0 else
                     carry[:, :, 0:1])))

            def pf(p, ch):
                return plane(M2, p).rearrange(
                    "p k (c l) -> p k c l", l=L)[:, :, ch:ch + 1, L - 1:L]

            def wv3(buf, ch):
                return buf[:, :, ch:ch + 1].unsqueeze(3)

            s0 = tq0[:].rearrange("p k (c l) -> p k c l", l=L)[:, :, 0:1, 0:1]
            s1 = tq1[:].rearrange("p k (c l) -> p k c l", l=L)[:, :, 0:1, 0:1]

            @bl.vector
            def _(v, q4=q4):
                c.w(v, "V", "vS", em[("scan", q4)])
                v.tensor_copy(wb_[:, :, 0:1],
                              (v1_sb[:].unsqueeze(2) if q4 == 0 else
                               carry[:, :, 1:2]))
                for ch in range(1, NCH + 1):
                    wdst = ((wa, wb_) if ch < NCH else (None, None))
                    v.tensor_tensor(s0, pf(0, ch - 1), wv3(wa, ch - 1),
                                    ALU.mult)
                    v.tensor_tensor(s1, pf(1, ch - 1), wv3(wb_, ch - 1),
                                    ALU.mult)
                    dst0 = (wv3(wa, ch) if ch < NCH
                            else carry[:, :, 0:1].unsqueeze(3))
                    v.tensor_tensor(dst0, s0, s1, ALU.add)
                    v.tensor_tensor(s0, pf(2, ch - 1), wv3(wa, ch - 1),
                                    ALU.mult)
                    v.tensor_tensor(s1, pf(3, ch - 1), wv3(wb_, ch - 1),
                                    ALU.mult)
                    dst1 = (wv3(wb_, ch) if ch < NCH
                            else carry[:, :, 1:2].unsqueeze(3))
                    ii = v.tensor_tensor(dst1, s0, s1, ALU.add)
                em[("comb", q4)] = c.inc("vS", ii)

            @bl.vector
            def _(v, q4=q4):
                c.w(v, "V", "vS", em[("comb", q4)])
                if q4 >= 1:
                    c.w(v, "V", "tS", em[("pmm", q4 - 1, 7)])

                def m4(p):
                    return plane(M2, p).rearrange("p k (c l) -> p k c l", l=L)

                wab = wa[:].unsqueeze(3).to_broadcast([128, 2, NCH, L])
                wbb = wb_[:].unsqueeze(3).to_broadcast([128, 2, NCH, L])
                t4a = tq0[:].rearrange("p k (c l) -> p k c l", l=L)
                t4b = tq1[:].rearrange("p k (c l) -> p k c l", l=L)
                v.tensor_tensor(t4a, m4(0), wab, ALU.mult)
                v.tensor_tensor(t4b, m4(1), wbb, ALU.mult)
                v.tensor_tensor(hb0[:].rearrange("p k (c l) -> p k c l", l=L),
                                t4a, t4b, ALU.add)
                v.tensor_tensor(t4a, m4(2), wab, ALU.mult)
                v.tensor_tensor(t4b, m4(3), wbb, ALU.mult)
                ii = v.tensor_tensor(
                    hb1[:].rearrange("p k (c l) -> p k c l", l=L),
                    t4a, t4b, ALU.add)
                em[("happ", q4)] = c.inc("vS", ii)
                em[("scan_done", q4)] = em[("happ", q4)]

            for oc in range(8):
                bank = PS[4 + oc % 2]

                @bl.tensor
                def _(t, q4=q4, oc=oc, bank=bank):
                    c.w(t, "T", "dW", em["wpld"])
                    c.w(t, "T", "vS", em[("happ", q4)])
                    if oc >= 2 or q4 > 0:
                        c.w(t, "T", "vS", em.get(("pev", q4, oc - 2),
                                                 em.get(("pev", q4 - 1, 7),
                                                        0)))
                    for chn, hbuf in ((0, hb0), (1, hb0), (2, hb1), (3, hb1)):
                        kh = chn % 2
                        ii = t.matmul(bank[:], wp_sb[:, chn, sl(oc, 128)],
                                      hbuf[:, kh, :],
                                      start=(chn == 0), stop=(chn == 3))
                    em[("pmm", q4, oc)] = c.inc("tS", ii)

                @bl.vector
                def _(v, q4=q4, oc=oc, bank=bank):
                    c.w(v, "V", "tS", em[("pmm", q4, oc)])
                    c.w(v, "V", "vS", ab1_mark)
                    v.scalar_tensor_tensor(
                        tq0[:, 0, :], bank[:], os_t[:, oc:oc + 1],
                        BB[:, sl(q4, 512)], ALU.mult, ALU.add)
                    v.tensor_tensor(tq1[:, 0, :], X[:, oc, sl(q4, 512)],
                                    AB[:, sl(q4, 512)], ALU.mult)
                    ii = v.tensor_tensor(X[:, oc, sl(q4, 512)],
                                         tq0[:, 0, :], tq1[:, 0, :], ALU.add)
                    em[("pev", q4, oc)] = c.inc("vS", ii)


def _host_prep(inp):
    f32 = np.float32
    x = np.asarray(inp["L0_attn_a_hidden_states"], f32)
    x_cores = [np.ascontiguousarray(x[b].T) for b in range(8)]

    Wqkv = np.asarray(inp["L0_attn_a_Wqkv_kernel"], f32)
    qperm = np.empty(1024, np.int64)
    for s in range(4):
        for a in range(4):
            h = 4 * s + a
            for j in range(32):
                qperm[s * 128 + a * 32 + j] = h * 64 + 2 * j
                qperm[512 + s * 128 + a * 32 + j] = h * 64 + 2 * j + 1
    perm_qk = np.concatenate([qperm, 1024 + qperm])
    wqkv_qk = np.ascontiguousarray(Wqkv[:, perm_qk])
    wqkv_v = np.ascontiguousarray(Wqkv[:, 2048:])
    bias = np.asarray(inp["L0_attn_a_Wqkv_bias"], f32)
    qk_bias = np.ascontiguousarray(bias[perm_qk].reshape(16, 128).T)
    v_bias = np.ascontiguousarray(np.tile(bias[None, 2048:], (128, 1)))

    rope = np.asarray(inp["L0_attn_a_rope_cache"], f32)
    cosw = np.tile(np.tile(rope[:, :, 0].T, (1, 8)), (4, 1))
    sinw = np.tile(np.tile(rope[:, :, 1].T, (1, 8)), (4, 1))

    pem = np.zeros((4, 128, 128), f32)
    for j in range(32):
        pem[0, j, j] = 1.0
        pem[0, 32 + j, 64 + j] = 1.0
        pem[1, 64 + j, j] = 1.0
        pem[1, 96 + j, 64 + j] = 1.0
        pem[2, j, 32 + j] = 1.0
        pem[2, 32 + j, 96 + j] = 1.0
        pem[3, 64 + j, 32 + j] = 1.0
        pem[3, 96 + j, 96 + j] = 1.0

    wo = np.ascontiguousarray(np.asarray(inp["L0_attn_a_Wo_kernel"], f32))
    wo_bias = np.ascontiguousarray(
        np.asarray(inp["L0_attn_a_Wo_bias"], f32).reshape(8, 128).T)

    ai = np.arange(128)
    m = np.zeros((128, 384), f32)
    m[:, 0:128] = np.where(ai[None, :] >= ai[:, None], 0.0, NEG)
    m[:, 256:384] = np.where(ai[None, :] <= ai[:, None], 0.0, NEG)

    gate_w = np.asarray(inp["L0_ffn_gate_w"], f32)
    gbias = np.tile(np.asarray(inp["L0_ffn_gate_bias"], f32)[None, None, :],
                    (128, 16, 1)).reshape(128, 256)
    epsv = np.tile((np.arange(15, -1, -1, dtype=f32) * 1e-26)[None, None, :],
                   (128, 16, 1)).reshape(128, 256)
    epsg = np.tile((np.arange(3, -1, -1, dtype=f32) * 1e-26)[None, None, :],
                   (128, 16, 1)).reshape(128, 64)
    gp = np.asarray(inp["L0_ffn_gate_proj"], f32)
    upw = np.asarray(inp["L0_ffn_up_proj"], f32)
    dn = np.asarray(inp["L0_ffn_down_proj"], f32)
    sgw = np.asarray(inp["L0_ffn_shared_gate_w"], f32)
    suw = np.asarray(inp["L0_ffn_shared_up_w"], f32)
    sdw = np.asarray(inp["L0_ffn_shared_down_w"], f32)
    gpT = np.ascontiguousarray(np.concatenate(
        [gp.transpose(0, 2, 1), sgw[:1408].T[None], sgw[1408:].T[None]], 0))
    upT = np.ascontiguousarray(np.concatenate(
        [upw.transpose(0, 2, 1), suw[:1408].T[None], suw[1408:].T[None]], 0))
    dnT = np.ascontiguousarray(np.concatenate(
        [dn.transpose(0, 2, 1), sdw[:, :1408].T[None],
         sdw[:, 1408:].T[None]], 0))

    Wm = np.asarray(inp["L0_attn_b_W_m"], f32)
    perm_m = np.empty(1024, np.int64)
    for plane_i in range(4):
        for k in range(256):
            perm_m[plane_i * 256 + k] = 4 * k + plane_i
    wmT = np.ascontiguousarray(Wm.T[:, perm_m])
    vv = np.asarray(inp["L0_attn_b_v_vec"], f32)
    v0t = np.ascontiguousarray(vv[:, 0, 0].reshape(2, 128).T)
    v1t = np.ascontiguousarray(vv[:, 1, 0].reshape(2, 128).T)
    Wp = np.asarray(inp["L0_attn_b_W_proj"], f32)
    perm_p = np.empty(512, np.int64)
    for ch in range(4):
        cc, kh = ch // 2, ch % 2
        for kp in range(128):
            perm_p[ch * 128 + kp] = 2 * (kh * 128 + kp) + cc
    wpT = np.ascontiguousarray(Wp.T[perm_p])
    os_t = np.ascontiguousarray(
        np.asarray(inp["L0_attn_b_out_scale"], f32).reshape(8, 128).T)

    shared = dict(
        wqkv_qk=wqkv_qk, wqkv_v=wqkv_v, qk_bias=qk_bias, v_bias=v_bias,
        ropeC=np.ascontiguousarray(cosw).astype(BF),
        ropeS=np.ascontiguousarray(sinw).astype(BF),
        pemats=pem.astype(BF), wo=wo, wo_bias=wo_bias, mask384=m,
        mask_bf=m.astype(BF),
        id_bf=np.eye(128, dtype=BF), id_f32=np.eye(128, dtype=f32),
        ones_col_bf=np.ones((128, 1), BF),
        ones_row_bf=np.ones((128, 128), BF),
        ones_row_f32=np.ones((128, 128), f32),
        gwT=np.ascontiguousarray(gate_w.T), gbias=gbias, epsv=epsv, epsg=epsg,
        gpT=gpT, upT=upT, dnT=dnT,
        wmT=wmT, v0t=v0t, v1t=v1t, wpT=wpT, os_t=os_t,
        epsc=np.tile(np.array([[EPS_LN, EPS_RMS]], f32), (128, 1)),
    )
    return shared, x_cores


def kernel(**inputs):
    shared, x_cores = _host_prep(inputs)
    nc = build(_UPTO)
    in_maps = []
    for cid in range(8):
        mcore = dict(shared)
        mcore["x"] = x_cores[cid]
        in_maps.append(mcore)
    res = run_bass_kernel_spmd(nc, in_maps, core_ids=list(range(8)))
    out = np.stack([np.asarray(r["out"], np.float32).T for r in res.results],
                   axis=0)
    return out
